# version 16
# baseline (speedup 1.0000x reference)
import contextlib
import numpy as np

import concourse.bass as bass
import concourse.mybir as mybir
from concourse.bass_utils import run_bass_kernel_spmd

F32 = mybir.dt.float32
BF16 = mybir.dt.bfloat16
I32 = mybir.dt.int32
AF = mybir.ActivationFunctionType
ALU = mybir.AluOpType
AX = mybir.AxisListType

D = 1024
L = 2048
N = 16
FF = 4096
NB = D // 128
FB = FF // 128
TC = 512
NCH = L // TC
EPS = 1e-5
LORA_SCALE = 2.0
QMAX = 127.0
NDA = 8
NDBX = 4


def rep_row(src_ap, width):
    pitch = src_ap.ap[0][0]
    return bass.AP(src_ap.tensor, src_ap.offset, [[pitch, 1], [0, 128], [1, width]])


def build_program():
    nc = bass.Bass("TRN2", target_bir_lowering=False,
                   detect_race_conditions=False)

    def din(name, shape):
        return nc.dram_tensor(name, shape, F32, kind="ExternalInput").ap()

    x0_h = din("x0", [D, L])
    wd_h = din("wd", [D, D])
    wbc_h = din("wbc", [D, 32])
    afc_h = din("afc", [D, N])
    bfcT_h = din("bfcT", [N, FF])
    wfcT_h = din("wfcT", [D, FF])
    wprojT_h = din("wprojT", [FF, D])
    aprojT_h = din("aprojT", [FF, N])
    bprojT_h = din("bprojT", [N, D])
    alog_h = din("alog", [D, N])
    gam_h = din("gam", [D, 1])
    bet_h = din("bet", [D, 1])
    dskip_h = din("dskip", [D, 1])
    bd_h = din("bd", [D, 1])
    bbc_h = din("bbc", [32, 1])
    bfc_h = din("bfc", [FF, 1])
    bproj_h = din("bproj", [D, 1])
    ident_h = din("ident", [128, 128])

    out_h = nc.dram_tensor("out", [D, L], F32, kind="ExternalOutput").ap()
    qfc_h = nc.dram_tensor("qfc_scr", [D, FF], BF16, kind="Internal").ap()
    qproj_h = nc.dram_tensor("qproj_scr", [FF, D], BF16, kind="Internal").ap()

    cnt = {}

    def inc(instr, sem, n=1):
        instr.then_inc(sem, n)
        cnt[sem.name] = cnt.get(sem.name, 0) + n
        return cnt[sem.name]

    def val(sem):
        return cnt.get(sem.name, 0)

    with contextlib.ExitStack() as ctx:
        E = ctx.enter_context
        sb = lambda name, shape, dt=F32: E(nc.sbuf_tensor(name, shape, dt))

        wdT = sb("s_wdT", [128, NB * D], BF16)
        wbc = sb("s_wbc", [128, NB * 32], BF16)
        afc = sb("s_afc", [128, NB * N], BF16)
        bfcT = sb("s_bfcT", [N, FF], BF16)
        aprojT = sb("s_aprojT", [128, FB * N], BF16)
        bprojT = sb("s_bprojT", [N, D], BF16)
        alog = sb("s_alog", [128, NB * N])
        aneg = sb("s_aneg", [128, NB * N])
        gam = sb("s_gam", [128, NB])
        bet = sb("s_bet", [128, NB])
        dskip = sb("s_dskip", [128, NB])
        bd = sb("s_bd", [128, NB])
        bbc = sb("s_bbc", [32, 1])
        bfc = sb("s_bfc", [128, FB])
        bproj = sb("s_bproj", [128, NB])
        ones_bf = sb("s_ones", [128, 1], BF16)
        ident_bf = sb("s_ident", [128, 128], BF16)
        onesf = sb("s_onesf", [128, 1])
        z1 = sb("s_z1", [128, 1])
        epsb = sb("s_epsb", [1, 1])
        amax_col = sb("s_amaxc", [128, 2])
        amax_rows = sb("s_amaxrows", [128, 40])
        amax_row = sb("s_amaxr", [1, 256])
        qsc = sb("s_qsc", [1, 4])
        qsc_b = sb("s_qscb", [128, 4])
        stat = sb("s_stat", [1, 5 * TC])
        bca = sb("s_bca", [128, 4 * TC])
        bcm = sb("s_bcm", [32, L], BF16)
        midfc = sb("s_midfc", [N, L], BF16)
        midpj = sb("s_midpj", [N, TC], BF16)
        carry = sb("s_carry", [128, NB * N])
        x2c = sb("s_x2c", [128, 2 * TC])
        x1f = sb("s_x1f", [128, NB * L], BF16)

        A1 = sb("s_arena1", [128, 20480])
        A2 = sb("s_arena2", [128, 2 * NB * TC])

        a1 = A1[:]
        wstg = a1[:, 0:8192]
        amax_acc = a1[:, 8192:12288]
        qi32 = a1[:, 12288:16384].bitcast(I32)
        qbf = a1[:, 16384:20480].bitcast(BF16)
        brep = a1[:, 0:4096].bitcast(BF16)
        crep = a1[:, 4096:8192].bitcast(BF16)
        hbuf = a1[:, 8192:12288].bitcast(BF16)
        xn2c = a1[:, 12288:14336].bitcast(BF16)
        dltF = a1[:, 14336:16384].bitcast(BF16)
        uu = a1[:, 16384:16640].bitcast(BF16)
        tln = a1[:, 16640:17152]
        dbx2 = a1[:, 17152:18176].bitcast(BF16)
        da = a1[:, 18176:20224].bitcast(BF16)

        h1c = a1[:, 0:8192].bitcast(BF16)
        xnfc = a1[:, 8192:10240].bitcast(BF16)
        fcw = a1[:, 10240:14336].bitcast(BF16)
        pjw = a1[:, 14336:18432].bitcast(BF16)

        a2 = A2[:]
        stage = a2

        def x0t(slot, k):
            return a2[:, (slot * NB + k) * TC:(slot * NB + k + 1) * TC]

        pmm = [E(nc.psum_tensor("p_mm0", [128, 512], F32)),
               E(nc.psum_tensor("p_mm1", [128, 512], F32))]
        pbc = E(nc.psum_tensor("p_bc", [32, 512], F32))
        pmid = E(nc.psum_tensor("p_mid", [N, 512], F32))
        pln = E(nc.psum_tensor("p_ln", [1, 2 * TC], F32))
        pyc = E(nc.psum_tensor("p_yc", [128, 512], F32))

        qC = E(nc.semaphore("qC"))
        qW = [E(nc.semaphore("qW0")), E(nc.semaphore("qW1"))]
        qQ = [E(nc.semaphore("qQ0")), E(nc.semaphore("qQ1"))]
        qR = E(nc.semaphore("qR"))
        qX = [E(nc.semaphore("qX0")), E(nc.semaphore("qX1"))]
        qB = E(nc.semaphore("qB"))
        qFW = E(nc.semaphore("qFW"))
        qPW = E(nc.semaphore("qPW"))
        qO = [E(nc.semaphore("qO0")), E(nc.semaphore("qO1"))]
        sV = E(nc.semaphore("sV"))
        sA = E(nc.semaphore("sA"))
        sP = E(nc.semaphore("sP"))
        sG = E(nc.semaphore("sG"))

        sqsf = x2c[:, 0:TC]

        block = E(nc.Block())

        def SR(i):
            return stat[0:1, i * TC:(i + 1) * TC]

        @block.sync
        def _(sync):
            ctxnc = nc.allow_non_contiguous_dma(reason="tiny per-dblock vectors")
            ctxnc.__enter__()
            inc(sync.dma_start(out=stage[:, 0:NB * D].rearrange("p (b d) -> p b d", b=NB), in_=wd_h.rearrange("(b p) d -> p b d", p=128)), qC, 16)
            inc(sync.dma_start(out=alog[:].rearrange("p (b n) -> p b n", b=NB), in_=alog_h.rearrange("(b p) n -> p b n", p=128)), qC, 16)
            inc(sync.dma_start(out=gam[:].unsqueeze(2), in_=gam_h.rearrange("(b p) o -> p b o", p=128)), qC, 16)
            inc(sync.dma_start(out=bet[:].unsqueeze(2), in_=bet_h.rearrange("(b p) o -> p b o", p=128)), qC, 16)
            inc(sync.dma_start(out=dskip[:].unsqueeze(2), in_=dskip_h.rearrange("(b p) o -> p b o", p=128)), qC, 16)
            inc(sync.dma_start(out=bd[:].unsqueeze(2), in_=bd_h.rearrange("(b p) o -> p b o", p=128)), qC, 16)
            inc(sync.dma_start(out=bbc[:], in_=bbc_h), qC, 16)
            inc(sync.dma_start(out=bfc[:].unsqueeze(2), in_=bfc_h.rearrange("(b p) o -> p b o", p=128)), qC, 16)
            inc(sync.dma_start(out=bproj[:].unsqueeze(2), in_=bproj_h.rearrange("(b p) o -> p b o", p=128)), qC, 16)
            ctxnc.__exit__(None, None, None)
        v_c0 = val(qC)

        @block.vector
        def _(vector):
            vector.wait_ge(qC, v_c0)
            vector.memset(ones_bf[:], 1.0)
            vector.memset(onesf[:], 1.0)
            vector.memset(z1[:], 0.0)
            vector.memset(epsb[:], EPS)
            vector.memset(carry[:], 0.0)
            inc(vector.tensor_copy(out=wdT[:], in_=stage[:, 0:NB * D]), sV, 1)
        v_wdT = val(sV)

        @block.scalar
        def _(scalar):
            scalar.wait_ge(sV, v_wdT)
            inc(scalar.activation(out=stage[:, 0:NB * N], in_=alog[:], func=AF.Exp), sA, 1)
        v_aexp = val(sA)

        @block.vector
        def _(vector):
            vector.wait_ge(sA, v_aexp)
            inc(vector.tensor_scalar_mul(out=aneg[:], in0=stage[:, 0:NB * N], scalar1=-1.0), sV, 1)
        v_aneg = val(sV)

        @block.sync
        def _(sync):
            sync.wait_ge(sV, v_aneg)
            inc(sync.dma_start(out=stage[:, 0:NB * 32].rearrange("p (b d) -> p b d", b=NB), in_=wbc_h.rearrange("(b p) d -> p b d", p=128)), qC, 16)
            inc(sync.dma_start(out=stage[:, NB * 32:NB * 48].rearrange("p (b n) -> p b n", b=NB), in_=afc_h.rearrange("(b p) n -> p b n", p=128)), qC, 16)
            inc(sync.dma_start(out=stage[:, 512:512 + FB * N].rearrange("p (b n) -> p b n", b=FB), in_=aprojT_h.rearrange("(b p) n -> p b n", p=128)), qC, 16)
            inc(sync.dma_start(out=stage[0:N, 1024:1024 + FF], in_=bfcT_h), qC, 16)
            inc(sync.dma_start(out=stage[0:N, 5120:5120 + D], in_=bprojT_h), qC, 16)
            inc(sync.dma_start(out=stage[:, 6144:6144 + 128], in_=ident_h), qC, 16)
        v_c1 = val(qC)

        @block.vector
        def _(vector):
            vector.wait_ge(qC, v_c1)
            vector.tensor_copy(out=wbc[:], in_=stage[:, 0:NB * 32])
            vector.tensor_copy(out=afc[:], in_=stage[:, NB * 32:NB * 48])
            vector.tensor_copy(out=aprojT[:], in_=stage[:, 512:512 + FB * N])
            vector.tensor_scalar_mul(out=bfcT[:], in0=stage[0:N, 1024:1024 + FF], scalar1=LORA_SCALE)
            vector.tensor_scalar_mul(out=bprojT[:], in0=stage[0:N, 5120:5120 + D], scalar1=LORA_SCALE)
            inc(vector.tensor_copy(out=ident_bf[:], in_=stage[:, 6144:6144 + 128]), sV, 1)
        v_consts = val(sV)

        row_ms = []
        for wi, (w_h, klen, rows) in enumerate(((wfcT_h, FF, NB), (wprojT_h, D, FB))):
            for r in range(rows):
                g = len(row_ms)
                slot = g % 2

                @block.sync
                def _(sync, g=g, slot=slot, w_h=w_h, klen=klen, r=r):
                    if g >= 2:
                        sync.wait_ge(sV, row_ms[g - 2])
                    if g == 0:
                        sync.wait_ge(sV, v_consts)
                    inc(sync.dma_start(out=wstg[:, slot * FF:slot * FF + klen],
                                       in_=w_h[r * 128:(r + 1) * 128, :]), qW[slot], 16)
                v_ld = val(qW[slot])

                @block.vector
                def _(vector, g=g, slot=slot, klen=klen, v_ld=v_ld):
                    vector.wait_ge(qW[slot], v_ld)
                    row_ms.append(inc(vector.tensor_reduce(
                        out=amax_rows[:, g:g + 1],
                        in_=wstg[:, slot * FF:slot * FF + klen],
                        axis=AX.X, op=ALU.max, apply_absolute_value=True), sV, 1))

        @block.vector
        def _(vector):
            vector.wait_ge(sV, row_ms[-1])
            vector.tensor_reduce(out=amax_col[:, 0:1], in_=amax_rows[:, 0:NB],
                                 axis=AX.X, op=ALU.max)
            inc(vector.tensor_reduce(out=amax_col[:, 1:2], in_=amax_rows[:, NB:40],
                                     axis=AX.X, op=ALU.max), sV, 1)
        v_amax = val(sV)

        @block.sync
        def _(sync):
            sync.wait_ge(sV, v_amax)
            with nc.allow_non_contiguous_dma(reason="two 128-element transposes, once"):
                inc(sync.dma_start(out=amax_row[0:1, 0:128], in_=amax_col[:, 0:1]), qC, 16)
                inc(sync.dma_start(out=amax_row[0:1, 128:256], in_=amax_col[:, 1:2]), qC, 16)
        v_arow = val(qC)

        @block.vector
        def _(vector):
            vector.wait_ge(qC, v_arow)
            vector.tensor_reduce(out=qsc[0:1, 0:1], in_=amax_row[0:1, 0:128], axis=AX.X, op=ALU.max)
            vector.tensor_reduce(out=qsc[0:1, 2:3], in_=amax_row[0:1, 128:256], axis=AX.X, op=ALU.max)
            inc(vector.engine_nop(), sV, 1)
        v_maxv = val(sV)

        @block.vector
        def _(vector):
            vector.wait_ge(sV, v_maxv)
            vector.tensor_scalar_mul(out=qsc[0:1, 0:1], in0=qsc[0:1, 0:1], scalar1=1.0 / QMAX)
            inc(vector.tensor_scalar_mul(out=qsc[0:1, 2:3], in0=qsc[0:1, 2:3], scalar1=1.0 / QMAX), sV, 1)
        v_scale = val(sV)

        @block.vector
        def _(vector):
            vector.wait_ge(sV, v_scale)
            inc(vector.reciprocal(out=qsc[0:1, 1:2], in_=qsc[0:1, 0:1]), sV, 1)
            inc(vector.reciprocal(out=qsc[0:1, 3:4], in_=qsc[0:1, 2:3]), sV, 1)
        v_inv = val(sV)

        @block.sync
        def _(sync):
            sync.wait_ge(sV, v_inv)
            inc(sync.dma_start(out=qsc_b[:].unsqueeze(1), in_=rep_row(qsc[0:1, 0:4], 4)), qC, 16)
        v_qscb = val(qC)

        q_dve = []
        q_st = []
        for wi, (w_h, q_h, klen, rows) in enumerate(
                ((wfcT_h, qfc_h, FF, NB), (wprojT_h, qproj_h, D, FB))):
            inv_ap = qsc_b[:, 2 * wi + 1:2 * wi + 2]
            sc_ap = qsc_b[:, 2 * wi:2 * wi + 1]
            for r in range(rows):
                g = len(q_dve)
                slot = g % 2

                @block.sync
                def _(sync, g=g, slot=slot, w_h=w_h, klen=klen, r=r):
                    if g >= 2:
                        sync.wait_ge(sV, q_dve[g - 2])
                    else:
                        sync.wait_ge(sV, row_ms[-1])
                    inc(sync.dma_start(out=wstg[:, slot * FF:slot * FF + klen],
                                       in_=w_h[r * 128:(r + 1) * 128, :]), qW[slot], 16)
                v_ld = val(qW[slot])

                @block.vector
                def _(vector, g=g, slot=slot, klen=klen, v_ld=v_ld,
                      inv_ap=inv_ap, sc_ap=sc_ap):
                    vector.wait_ge(qW[slot], v_ld)
                    vector.wait_ge(qC, v_qscb)
                    if g >= 2:
                        vector.wait_ge(qQ[slot], q_st[g - 2])
                    w = wstg[:, slot * FF:slot * FF + klen]
                    vector.tensor_scalar(out=qi32[:, 0:klen], in0=w, scalar1=inv_ap,
                                         scalar2=None, op0=ALU.mult)
                    q_dve.append(inc(vector.tensor_scalar(
                        out=qbf[:, slot * FF:slot * FF + klen], in0=qi32[:, 0:klen],
                        scalar1=sc_ap, scalar2=None, op0=ALU.mult), sV, 1))

                @block.sync
                def _(sync, g=g, slot=slot, q_h=q_h, klen=klen, r=r):
                    sync.wait_ge(sV, q_dve[g])
                    q_st.append(inc(sync.dma_start(
                        out=q_h[r * 128:(r + 1) * 128, :],
                        in_=qbf[:, slot * FF:slot * FF + klen]), qQ[slot], 16))
        v_quant_q0 = val(qQ[0])
        v_quant_q1 = val(qQ[1])
        v_quant_v = val(sV)

        CB = bca[:, 0:TC]
        ALB = bca[:, TC:2 * TC]
        P1B = bca[:, 2 * TC:3 * TC]
        Q1B = bca[:, 3 * TC:4 * TC]

        v_x1_done = []
        v_x1_list = []
        v_yred_list = []
        v_yredseg_list = []
        v_exp_list = []
        v_u_list = []
        v_ldx_by_chunk = []
        sums_base = {}
        v_m1_by_chunk = {}
        v_bcev_prev = 0
        v_scan_last = 0
        mA_sq_saved = None

        def emit_x0_load(c):
            slot = c % 2

            @block.sync
            def _(sync, c=c, slot=slot):
                if c >= 2:
                    sync.wait_ge(sV, v_x1_done[c - 2])
                dst = a2[:, slot * NB * TC:(slot + 1) * NB * TC].rearrange(
                    "p (k t) -> p k t", k=NB)
                srcv = x0_h.rearrange("(k p) t -> p k t", p=128)[:, :, c * TC:(c + 1) * TC]
                inc(sync.dma_start(out=dst, in_=srcv), qX[slot], 16)
            v_ldx_by_chunk.append(val(qX[slot]))

        emit_x0_load(0)
        emit_x0_load(1)

        for c in range(NCH):
            slot = c % 2
            v_scan_last_entry = v_scan_last

            if c == 0:
                P0 = val(sA)
                mA_sq = [P0 + 1 + k for k in range(NB)]
                pe_base0 = val(sP)

                @block.scalar
                def _(scalar, pe_base0=pe_base0):
                    scalar.wait_ge(qX[0], v_ldx_by_chunk[0])
                    scalar.wait_ge(sV, v_quant_v)
                    scalar.wait_ge(qQ[0], v_quant_q0)
                    scalar.wait_ge(qQ[1], v_quant_q1)
                    for k in range(NB):
                        if k >= 2:
                            scalar.wait_ge(sP, pe_base0 + k - 1)
                        inc(scalar.activation(
                            out=x2c[:, (k % 2) * TC:(k % 2 + 1) * TC],
                            in_=x0t(0, k), func=AF.Square), sA, 1)

                sums_base[0] = val(sP)

                @block.tensor
                def _(tensor, mA_sq=mA_sq):
                    tensor.wait_ge(sV, v_consts)
                    for k in range(NB):
                        tensor.wait_ge(sA, mA_sq[k])
                        tensor.matmul(out=pln[0:1, 0:TC], lhsT=onesf[:],
                                      rhs=x0t(0, k),
                                      start=(k == 0), stop=(k == NB - 1))
                        inc(tensor.matmul(out=pln[0:1, TC:2 * TC], lhsT=onesf[:],
                                          rhs=x2c[:, (k % 2) * TC:(k % 2 + 1) * TC],
                                          start=(k == 0), stop=(k == NB - 1)), sP, 1)
            else:
                mA_sq = mA_sq_saved

            A0 = val(sA)
            nsq = NB if c + 1 < NCH else 0
            mA_sq_next = [A0 + 1 + k for k in range(nsq)]
            B0 = A0 + nsq
            mA_sqrt = B0 + 1
            mA_gb = [B0 + 2 + k for k in range(NB)]
            mA_bcev = B0 + 11
            mA_exp = [B0 + 12 + k for k in range(NB)]
            mA_ln = [B0 + 20 + k for k in range(NB)]
            mA_da = [[B0 + 28 + 16 * k + n for n in range(N)] for k in range(NB)]
            A_end = B0 + 28 + 16 * NB - 1
            mA_sq_saved = mA_sq_next

            v_sums = sums_base[c] + NB

            @block.vector
            def _(vector, v_sums=v_sums):
                vector.wait_ge(sP, v_sums)
                vector.tensor_scalar_mul(out=SR(0), in0=pln[0:1, 0:TC], scalar1=1.0 / D)
                inc(vector.tensor_scalar_mul(out=SR(1), in0=pln[0:1, TC:2 * TC],
                                             scalar1=1.0 / D), sV, 1)
            v_m1 = val(sV)
            v_m1_by_chunk[c] = v_m1

            @block.vector
            def _(vector, v_m1=v_m1):
                vector.tensor_mul(out=SR(2), in0=SR(0), in1=SR(0))
                vector.tensor_sub(out=SR(1), in0=SR(1), in1=SR(2))
                vector.tensor_scalar_add(out=SR(2), in0=SR(1), scalar1=EPS)
                inc(vector.tensor_scalar(out=SR(3), in0=SR(1), scalar1=1.0 + EPS,
                                         scalar2=EPS * EPS, op0=ALU.mult,
                                         op1=ALU.add), sV, 1)
            v_v1 = val(sV)

            @block.vector
            def _(vector, mA_sqrt=mA_sqrt):
                vector.wait_ge(sA, mA_sqrt)
                vector.reciprocal(out=SR(2), in_=SR(2))
                inc(vector.reciprocal(out=SR(1), in_=SR(3)), sV, 1)
            v_rv = val(sV)

            @block.vector
            def _(vector, v_rv=v_rv):
                vector.wait_ge(sV, v_rv)
                vector.tensor_mul(out=SR(4), in0=SR(0), in1=SR(2))
                inc(vector.tensor_scalar_add(out=SR(3), in0=SR(2), scalar1=1.0), sV, 1)
            v_stats = val(sV)

            @block.sync
            def _(sync, c=c, v_stats=v_stats):
                sync.wait_ge(sV, v_stats)
                if c >= 1:
                    sync.wait_ge(sV, v_x1_done[c - 1])
                for j, si in ((0, 0), (1, 1)):
                    inc(sync.dma_start(out=bca[:, j * TC:(j + 1) * TC].unsqueeze(1),
                                       in_=rep_row(SR(si), TC)), qB, 16)

            v_bcast = val(qB) + 32

            v_xn2pre = []

            @block.vector
            def _(vector, c=c, slot=slot, v_bcast=v_bcast):
                vector.wait_ge(qB, v_bcast)
                if c == 0:
                    vector.wait_ge(qQ[0], v_quant_q0)
                    vector.wait_ge(qQ[1], v_quant_q1)
                else:
                    vector.wait_ge(sV, v_x1_done[c - 1])
                for k in range(NB):
                    vector.tensor_sub(out=tln, in0=x0t(slot, k), in1=CB)
                    v_xn2pre.append(inc(vector.tensor_tensor(
                        out=xn2c[:, k * TC:(k + 1) * TC], in0=tln, in1=ALB,
                        op=ALU.mult), sV, 1))

            if c + 1 < NCH:
                sums_base[c + 1] = val(sP)

                @block.tensor
                def _(tensor, c=c, mA_sq_next=mA_sq_next, v_m1=v_m1,
                      nslot=(c + 1) % 2):
                    tensor.wait_ge(sV, v_m1)
                    for k in range(NB):
                        tensor.wait_ge(sA, mA_sq_next[k])
                        tensor.matmul(out=pln[0:1, 0:TC], lhsT=onesf[:],
                                      rhs=x0t(nslot, k),
                                      start=(k == 0), stop=(k == NB - 1))
                        inc(tensor.matmul(out=pln[0:1, TC:2 * TC], lhsT=onesf[:],
                                          rhs=x2c[:, (k % 2) * TC:(k % 2 + 1) * TC],
                                          start=(k == 0), stop=(k == NB - 1)), sP, 1)

            @block.tensor
            def _(tensor, c=c, v_bcev_prev=v_bcev_prev, mA_gb=mA_gb):
                tensor.wait_ge(sA, mA_gb[NB - 1])
                if c >= 1:
                    tensor.wait_ge(sA, v_bcev_prev)
                for k in range(NB):
                    tensor.matmul(out=pbc[:, 0:TC], lhsT=wbc[:, k * 32:(k + 1) * 32],
                                  rhs=xn2c[:, k * TC:(k + 1) * TC],
                                  start=(k == 0), stop=(k == NB - 1))
                for k in range(NB):
                    ins = tensor.matmul(out=pmid[:, 0:TC], lhsT=afc[:, k * N:(k + 1) * N],
                                        rhs=xn2c[:, k * TC:(k + 1) * TC],
                                        start=(k == 0), stop=(k == NB - 1))
                    if k == NB - 1:
                        inc(ins, sP, 1)
            v_bcmm = val(sP)
            v_bcev = mA_bcev
            v_bcev_prev = v_bcev

            @block.sync
            def _(sync, c=c, v_scans_prev=(v_u_list[-1] + N if c >= 1 else 0),
                  v_bcev=v_bcev):
                sync.wait_ge(sA, v_bcev)
                if c >= 1:
                    sync.wait_ge(sV, v_scans_prev)
                for n in range(N):
                    inc(sync.dma_start(
                        out=brep[:, n * TC:(n + 1) * TC].unsqueeze(1),
                        in_=rep_row(bcm[n:n + 1, c * TC:(c + 1) * TC], TC)), qR, 16)
            v_reps = val(qR) + 16 * N

            v_dmm_list = []
            for k in range(NB):
                gk = c * NB + k
                mmslot = k % 2

                @block.tensor
                def _(tensor, c=c, k=k, gk=gk, mmslot=mmslot, mA_gb=mA_gb):
                    tensor.wait_ge(sA, mA_gb[NB - 1])
                    if gk >= 2:
                        tensor.wait_ge(sA, v_exp_list[gk - 2])
                    for kk in range(NB):
                        ins = tensor.matmul(
                            out=pmm[mmslot][:, 0:TC],
                            lhsT=wdT[:, kk * D + k * 128:kk * D + (k + 1) * 128],
                            rhs=xn2c[:, kk * TC:(kk + 1) * TC],
                            start=(kk == 0), stop=(kk == NB - 1))
                        if kk == NB - 1:
                            inc(ins, sP, 1)
                v_dmm_list.append(val(sP))
                v_exp_list.append(mA_exp[k])

            for k in range(NB):
                gk = c * NB + k

                @block.vector
                def _(vector, c=c, k=k, gk=gk, mA_ln=mA_ln):
                    vector.wait_ge(sA, mA_ln[k])
                    v_u_list.append(inc(vector.tensor_tensor(
                        out=uu, in0=dltF[:, k * TC:(k + 1) * TC],
                        in1=xn2c[:, k * TC:(k + 1) * TC],
                        op=ALU.mult), sV, 1))
                v_u = v_u_list[-1]

                v_dbx_list_k = []
                v_ym_list_k = []

                @block.gpsimd
                def _(gpsimd, c=c, k=k, gk=gk, v_u=v_u, v_reps=v_reps):
                    gpsimd.wait_ge(sV, v_u)
                    if k == 0:
                        gpsimd.wait_ge(qR, v_reps)
                    for n in range(N):
                        if n >= NDBX:
                            gpsimd.wait_ge(sV, v_u + n - NDBX + 1)
                        elif gk >= 1:
                            gpsimd.wait_ge(sV, v_u_list[gk - 1] + 17 + n - NDBX)
                        v_dbx_list_k.append(inc(gpsimd.tensor_tensor(
                            out=dbx2[:, (n % NDBX) * TC:(n % NDBX + 1) * TC],
                            in0=uu, in1=brep[:, n * TC:(n + 1) * TC],
                            op=ALU.mult), sG, 1))

                @block.gpsimd
                def _(gpsimd, c=c, k=k, gk=gk, v_u=v_u):
                    h3 = hbuf.rearrange("p (n t) -> p n t", n=N)
                    for n in range(N):
                        gpsimd.wait_ge(sV, v_u + n + 1)
                        gpsimd.tensor_tensor(
                            out=carry[:, k * N + n:k * N + n + 1],
                            in0=h3[:, n, TC - 1:TC], in1=z1[:],
                            op=ALU.add)
                        v_ym_list_k.append(inc(gpsimd.tensor_tensor(
                            out=hbuf[:, n * TC:(n + 1) * TC],
                            in0=hbuf[:, n * TC:(n + 1) * TC],
                            in1=crep[:, n * TC:(n + 1) * TC],
                            op=ALU.mult), sG, 1))

                @block.vector
                def _(vector, c=c, k=k, gk=gk, mA_da=mA_da, dbxms=v_dbx_list_k):
                    for n in range(N):
                        vector.wait_ge(sA, mA_da[k][n])
                        vector.wait_ge(sG, dbxms[n])
                        if gk >= 1:
                            vector.wait_ge(sP, v_yredseg_list[gk - 1][n])
                        inc(vector.tensor_tensor_scan(
                            out=hbuf[:, n * TC:(n + 1) * TC],
                            data0=da[:, (n % NDA) * TC:(n % NDA + 1) * TC],
                            data1=dbx2[:, (n % NDBX) * TC:(n % NDBX + 1) * TC],
                            initial=carry[:, k * N + n:k * N + n + 1],
                            op0=ALU.mult, op1=ALU.add), sV, 1)
                v_y = val(sV)
                v_scan_last = val(sG)

                yseg = []

                @block.tensor
                def _(tensor, c=c, k=k, gk=gk, ymms=v_ym_list_k):
                    if gk >= 1:
                        tensor.wait_ge(sV, v_x1_list[gk - 1])
                    for n in range(N):
                        tensor.wait_ge(sG, ymms[n])
                        ins = tensor.matmul(out=pyc[:, 0:TC], lhsT=ident_bf[:],
                                            rhs=hbuf[:, n * TC:(n + 1) * TC],
                                            start=(n == 0), stop=(n == N - 1))
                        yseg.append(inc(ins, sP, 1))
                v_yredseg_list.append(yseg)
                v_yred_list.append(yseg[-1])
                v_yred = v_yred_list[-1]

                @block.vector
                def _(vector, c=c, k=k, slot=slot, v_yred=v_yred):
                    vector.wait_ge(sP, v_yred)
                    x1s = x1f[:, k * L + c * TC:k * L + (c + 1) * TC]
                    vector.tensor_tensor(out=tln, in0=x0t(slot, k), in1=P1B, op=ALU.mult)
                    vector.tensor_sub(out=tln, in0=tln, in1=Q1B)
                    vector.tensor_add(out=tln, in0=tln, in1=pyc[:, 0:TC])
                    v_x1_list.append(inc(vector.scalar_tensor_tensor(
                        out=x1s, in0=xn2c[:, k * TC:(k + 1) * TC],
                        scalar=dskip[:, k:k + 1], in1=tln,
                        op0=ALU.mult, op1=ALU.add), sV, 1))
            v_x1_done.append(v_x1_list[-1])
            if c + 2 < NCH:
                emit_x0_load(c + 2)

            sq_guard_lo = sums_base[c] + NB - 1
            sq_guard_next = sums_base.get(c + 1)

            @block.scalar
            def _(scalar, c=c, slot=slot,
                  v_v1=v_v1, xn2pre=v_xn2pre, v_bcmm=v_bcmm, dmms=v_dmm_list,
                  v_scan_last_c=v_scan_last_entry, nsq=nsq, v_stats_c=v_stats,
                  sq_guard_lo=sq_guard_lo, sq_guard_next=sq_guard_next):
                if nsq:
                    scalar.wait_ge(qX[(slot + 1) % 2], v_ldx_by_chunk[c + 1])
                    for k in range(NB):
                        if k >= 2:
                            scalar.wait_ge(sP, sq_guard_next + k - 1)
                        else:
                            scalar.wait_ge(sP, sq_guard_lo + k)
                        inc(scalar.activation(
                            out=x2c[:, (k % 2) * TC:(k % 2 + 1) * TC],
                            in_=x0t((slot + 1) % 2, k),
                            func=AF.Square), sA, 1)
                scalar.wait_ge(sV, v_v1)
                scalar.activation(out=SR(2), in_=SR(2), func=AF.Sqrt)
                inc(scalar.activation(out=SR(3), in_=SR(3), func=AF.Sqrt), sA, 1)
                scalar.wait_ge(sV, v_stats_c)
                if c >= 1:
                    scalar.wait_ge(sV, v_x1_done[c - 1])
                for j, si in ((2, 3), (3, 4)):
                    inc(scalar.dma_start(out=bca[:, j * TC:(j + 1) * TC].unsqueeze(1),
                                         in_=rep_row(SR(si), TC)), qB, 16)
                for k in range(NB):
                    scalar.wait_ge(sV, xn2pre[k])
                    inc(scalar.activation(out=xn2c[:, k * TC:(k + 1) * TC],
                                          in_=xn2c[:, k * TC:(k + 1) * TC],
                                          func=AF.Identity, scale=gam[:, k:k + 1],
                                          bias=bet[:, k:k + 1]), sA, 1)
                scalar.wait_ge(sP, v_bcmm)
                inc(scalar.activation(out=bcm[:, c * TC:(c + 1) * TC],
                                      in_=pbc[:, 0:TC], func=AF.Identity,
                                      bias=bbc[:]), sA, 1)
                inc(scalar.activation(out=midfc[:, c * TC:(c + 1) * TC],
                                      in_=pmid[:, 0:TC], func=AF.Identity,
                                      scale=LORA_SCALE), sA, 1)
                for k in range(NB):
                    scalar.wait_ge(sP, dmms[k])
                    inc(scalar.activation(
                        out=dltF[:, k * TC:(k + 1) * TC],
                        in_=pmm[k % 2][:, 0:TC],
                        func=AF.Exp, bias=bd[:, k:k + 1]), sA, 1)
                for k in range(NB):
                    inc(scalar.activation(
                        out=dltF[:, k * TC:(k + 1) * TC],
                        in_=dltF[:, k * TC:(k + 1) * TC],
                        func=AF.Ln, bias=onesf[:]), sA, 1)
                if c >= 1:
                    scalar.wait_ge(sG, v_scan_last_c)
                for n in range(N):
                    inc(scalar.dma_start(
                        out=crep[:, n * TC:(n + 1) * TC].unsqueeze(1),
                        in_=rep_row(bcm[N + n:N + n + 1, c * TC:(c + 1) * TC], TC)), qR, 16)
                for k in range(NB):
                    gk = c * NB + k
                    for n in range(N):
                        if n >= NDA:
                            scalar.wait_ge(sV, v_u_list[gk] + n - NDA + 1)
                        elif gk >= 1:
                            scalar.wait_ge(sV, v_u_list[gk - 1] + 17 + n - NDA)
                        inc(scalar.activation(
                            out=da[:, (n % NDA) * TC:(n % NDA + 1) * TC],
                            in_=dltF[:, k * TC:(k + 1) * TC], func=AF.Exp,
                            scale=aneg[:, k * N + n:k * N + n + 1]), sA, 1)
            assert val(sA) == A_end, (c, val(sA), A_end)
            assert val(qR) == v_reps, (val(qR), v_reps)
            assert val(qB) == v_bcast, (val(qB), v_bcast)
        v_attn = val(sV)
        v_m1_prev = v_m1_by_chunk[NCH - 1]

        v_fc_pe_prev = 0
        v_midp_prev = 0
        v_bcev_last = v_bcev_prev
        fc_grp_ms = []
        pj_ms = []
        x2a_all = []
        x2st_all = []
        for c in range(NCH):
            F0 = val(sA)
            mF_sq = [F0 + 1 + k for k in range(NB)]
            mF_sqrt = F0 + 9
            mF_gelu = [F0 + 10 + f for f in range(FB)]
            mF_x2a = [F0 + 42 + k for k in range(NB)]
            F_end = F0 + 49

            v_pe_f_base = val(sP)

            @block.tensor
            def _(tensor, c=c, v_m1_prev=v_m1_prev, mF_sq=mF_sq):
                tensor.wait_ge(sV, v_m1_prev)
                for k in range(NB):
                    tensor.wait_ge(sA, mF_sq[k])
                    tensor.matmul(out=pln[0:1, 0:TC], lhsT=ones_bf[:],
                                  rhs=x1f[:, k * L + c * TC:k * L + (c + 1) * TC],
                                  start=(k == 0), stop=(k == NB - 1))
                    inc(tensor.matmul(out=pln[0:1, TC:2 * TC], lhsT=ones_bf[:],
                                      rhs=bca[:, 3 * TC:4 * TC].bitcast(BF16)[:, 0:TC],
                                      start=(k == 0), stop=(k == NB - 1)), sP, 1)
            v_sumsf = val(sP)

            @block.vector
            def _(vector, v_sumsf=v_sumsf):
                vector.wait_ge(sP, v_sumsf)
                vector.tensor_scalar_mul(out=SR(0), in0=pln[0:1, 0:TC], scalar1=1.0 / D)
                inc(vector.tensor_scalar_mul(out=SR(1), in0=pln[0:1, TC:2 * TC],
                                             scalar1=1.0 / D), sV, 1)
            v_mf = val(sV)
            v_m1_prev = v_mf

            @block.vector
            def _(vector, v_mf=v_mf):
                vector.tensor_mul(out=SR(2), in0=SR(0), in1=SR(0))
                vector.tensor_sub(out=SR(1), in0=SR(1), in1=SR(2))
                inc(vector.tensor_scalar_add(out=SR(1), in0=SR(1), scalar1=EPS), sV, 1)
            v_vf = val(sV)

            @block.vector
            def _(vector, mF_sqrt=mF_sqrt):
                vector.wait_ge(sA, mF_sqrt)
                inc(vector.reciprocal(out=SR(2), in_=SR(2)), sV, 1)
            v_rf = val(sV)

            @block.sync
            def _(sync, c=c, v_rf=v_rf, v_xnf_prev=(v_attn if c == 0 else 0)):
                sync.wait_ge(sV, v_rf)
                inc(sync.dma_start(out=bca[:, 0:TC].unsqueeze(1),
                                   in_=rep_row(SR(0), TC)), qB, 16)
                inc(sync.dma_start(out=bca[:, TC:2 * TC].unsqueeze(1),
                                   in_=rep_row(SR(2), TC)), qB, 16)
            v_bcf = val(qB)

            @block.vector
            def _(vector, c=c, v_bcf=v_bcf, v_fc_pe_prev=v_fc_pe_prev):
                vector.wait_ge(qB, v_bcf)
                if c >= 1:
                    vector.wait_ge(sP, v_fc_pe_prev)
                for k in range(NB):
                    vector.tensor_sub(out=tln, in0=x1f[:, k * L + c * TC:k * L + (c + 1) * TC],
                                      in1=CB)
                    inc(vector.tensor_tensor(
                        out=xnfc[:, k * TC:(k + 1) * TC],
                        in0=tln, in1=ALB, op=ALU.mult), sV, 1)
            v_xnf = val(sV)

            fcw_ld = []
            for g in range(FB // 4):
                wslot = g % 2
                guard = fc_grp_ms[-2] if len(fc_grp_ms) >= 2 else None

                @block.sync
                def _(sync, c=c, g=g, wslot=wslot, guard=guard, v_attn=v_attn):
                    if c == 0 and g <= 1:
                        sync.wait_ge(sV, v_attn)
                        sync.wait_ge(qQ[0], v_quant_q0)
                        sync.wait_ge(qQ[1], v_quant_q1)
                    if guard is not None:
                        sync.wait_ge(sP, guard)
                    dst = fcw[:, wslot * NB * 512:(wslot + 1) * NB * 512].rearrange(
                        "p (k t) -> p k t", k=NB)
                    srcv = qfc_h.rearrange("(k p) f -> p k f", p=128)[:, :, g * 512:(g + 1) * 512]
                    inc(sync.dma_start(out=dst, in_=srcv), qFW, 16)
                fcw_ld.append(val(qFW))

                for fi in range(4):
                    f = g * 4 + fi
                    bank = f % 2

                    @block.tensor
                    def _(tensor, c=c, g=g, f=f, fi=fi, wslot=wslot, bank=bank,
                          v_xnf=v_xnf, v_bcev_last=v_bcev_last, mF_gelu=mF_gelu,
                          x2a_prev=(x2a_all[-1] if x2a_all else None)):
                        tensor.wait_ge(qFW, fcw_ld[g])
                        tensor.wait_ge(sV, v_xnf)
                        tensor.wait_ge(sA, v_bcev_last)
                        if f >= 2:
                            tensor.wait_ge(sA, mF_gelu[f - 2])
                        elif x2a_prev is not None:
                            tensor.wait_ge(sA, x2a_prev)
                        for k in range(NB):
                            tensor.matmul(
                                out=pmm[bank][:, 0:TC],
                                lhsT=fcw[:, (wslot * NB + k) * 512 + fi * 128:
                                         (wslot * NB + k) * 512 + (fi + 1) * 128],
                                rhs=xnfc[:, k * TC:(k + 1) * TC],
                                start=(k == 0), stop=False)
                        inc(tensor.matmul(
                            out=pmm[bank][:, 0:TC],
                            lhsT=bfcT[:, f * 128:(f + 1) * 128],
                            rhs=midfc[:, c * TC:(c + 1) * TC],
                            start=False, stop=True), sP, 1)
                fc_grp_ms.append(val(sP))
            v_fc_pe_prev = val(sP)
            v_fcmm_list = list(range(v_pe_f_base + NB + 1, v_pe_f_base + NB + 1 + FB))

            @block.tensor
            def _(tensor, c=c, v_bcev_last=v_bcev_last,
                  v_midp_prev=v_midp_prev, mF_gelu=mF_gelu):
                tensor.wait_ge(sA, mF_gelu[FB - 1])
                if v_midp_prev:
                    tensor.wait_ge(sV, v_midp_prev)
                for fk in range(FB):
                    ins = tensor.matmul(
                        out=pmid[:, 0:TC],
                        lhsT=aprojT[:, fk * N:(fk + 1) * N],
                        rhs=h1c[:, fk * TC:(fk + 1) * TC],
                        start=(fk == 0), stop=(fk == FB - 1))
                inc(ins, sP, 1)
            v_midmm = val(sP)

            @block.vector
            def _(vector, c=c, v_midmm=v_midmm):
                vector.wait_ge(sP, v_midmm)
                inc(vector.tensor_scalar_mul(out=midpj[:], in0=pmid[:, 0:TC],
                                             scalar1=LORA_SCALE), sV, 1)
            v_midp = val(sV)
            v_midp_prev = v_midp

            v_x2_list = []
            for k in range(NB):
                wslot = k % 2
                guard = pj_ms[-2] if len(pj_ms) >= 2 else None
                bank = k % 2

                @block.sync
                def _(sync, c=c, k=k, wslot=wslot, guard=guard, v_attn=v_attn):
                    if c == 0 and k <= 1:
                        sync.wait_ge(sV, v_attn)
                        sync.wait_ge(qQ[0], v_quant_q0)
                        sync.wait_ge(qQ[1], v_quant_q1)
                    if guard is not None:
                        sync.wait_ge(sP, guard)
                    inc(sync.dma_start(
                        out=pjw[:, wslot * FB * 128:(wslot + 1) * FB * 128].rearrange("p (a d) -> p a d", a=FB),
                        in_=qproj_h[:, k * 128:(k + 1) * 128].rearrange(
                            "(a p) d -> p a d", p=128)), qPW, 16)
                v_pjw = val(qPW)

                @block.tensor
                def _(tensor, c=c, k=k, bank=bank, wslot=wslot,
                      v_pjw=v_pjw, v_midp=v_midp, mF_x2a=mF_x2a, mF_gelu=mF_gelu):
                    tensor.wait_ge(qPW, v_pjw)
                    tensor.wait_ge(sV, v_midp)
                    if k >= 2:
                        tensor.wait_ge(sA, mF_x2a[k - 2])
                    for fk in range(FB):
                        tensor.matmul(
                            out=pmm[bank][:, 0:TC],
                            lhsT=pjw[:, wslot * FB * 128 + fk * 128:
                                     wslot * FB * 128 + (fk + 1) * 128],
                            rhs=h1c[:, fk * TC:(fk + 1) * TC],
                            start=(fk == 0), stop=False)
                    inc(tensor.matmul(
                        out=pmm[bank][:, 0:TC],
                        lhsT=bprojT[:, k * 128:(k + 1) * 128],
                        rhs=midpj[:],
                        start=False, stop=True), sP, 1)
                v_pjmm = val(sP)
                pj_ms.append(v_pjmm)
                x2a_all.append(mF_x2a[k])

                @block.vector
                def _(vector, c=c, k=k, mF_x2a=mF_x2a):
                    vector.wait_ge(sA, mF_x2a[k])
                    xs = x2c[:, (k % 2) * TC:(k % 2 + 1) * TC]
                    inc(vector.tensor_add(
                        out=xs, in0=xs,
                        in1=x1f[:, k * L + c * TC:k * L + (c + 1) * TC]), sV, 1)
                v_x2_list.append(val(sV))
                v_x2 = v_x2_list[-1]

                @block.sync
                def _(sync, c=c, k=k, v_x2=v_x2):
                    sync.wait_ge(sV, v_x2)
                    x2st_all.append((qO[k % 2], inc(sync.dma_start(
                        out=out_h[k * 128:(k + 1) * 128, c * TC:(c + 1) * TC],
                        in_=x2c[:, (k % 2) * TC:(k % 2 + 1) * TC]), qO[k % 2], 16)))

            @block.scalar
            def _(scalar, c=c, F0=F0, v_vf=v_vf, fcmm=v_fcmm_list,
                  pjm=list(pj_ms[-NB:]), x2st=list(x2st_all),
                  v_attn=v_attn, v_x2l=v_x2_list, v_pe_f_base=v_pe_f_base):
                if c == 0:
                    scalar.wait_ge(sV, v_attn)
                for k in range(NB):
                    if k >= 1:
                        scalar.wait_ge(sP, v_pe_f_base + k)
                    inc(scalar.activation(
                        out=bca[:, 3 * TC:4 * TC].bitcast(BF16)[:, 0:TC],
                        in_=x1f[:, k * L + c * TC:k * L + (c + 1) * TC],
                        func=AF.Square), sA, 1)
                scalar.wait_ge(sV, v_vf)
                inc(scalar.activation(out=SR(2), in_=SR(1), func=AF.Sqrt), sA, 1)
                for f in range(FB):
                    scalar.wait_ge(sP, fcmm[f])
                    inc(scalar.activation(
                        out=h1c[:, f * TC:(f + 1) * TC],
                        in_=pmm[(f % 2)][:, 0:TC], func=AF.Gelu,
                        bias=bfc[:, f:f + 1]), sA, 1)
                for k in range(NB):
                    scalar.wait_ge(sP, pjm[k])
                    nst = len(x2st) - NB + k
                    if nst >= 2 and k >= 2:
                        sm, vv = x2st[nst - 2] if nst - 2 < len(x2st) else (None, None)
                    idx = len(x2st) - NB + k - 2
                    if idx >= 0:
                        smv = x2st[idx]
                        scalar.wait_ge(smv[0], smv[1])
                    inc(scalar.activation(
                        out=x2c[:, (k % 2) * TC:(k % 2 + 1) * TC],
                        in_=pmm[(k % 2)][:, 0:TC],
                        func=AF.Identity, bias=bproj[:, k:k + 1]), sA, 1)
            assert val(sA) == F_end + 1 - 1, (val(sA), F_end)


        v_out0 = val(qO[0])
        v_out1 = val(qO[1])

        @block.vector
        def _(vector):
            vector.wait_ge(qO[0], v_out0)
            vector.wait_ge(qO[1], v_out1)

    return nc


def kernel(**inputs):
    ins = {k: np.asarray(v) for k, v in inputs.items()}
    x = ins["L0_attn_x"]
    n_cores = 8

    common = dict(
        wd=np.ascontiguousarray(ins["L0_attn_W_delta"]),
        wbc=np.ascontiguousarray(
            np.concatenate([ins["L0_attn_W_B"], ins["L0_attn_W_C"]], axis=1)),
        afc=np.ascontiguousarray(ins["L0_ffn_A_fc"].T),
        bfcT=np.ascontiguousarray(ins["L0_ffn_B_fc"].T),
        wfcT=np.ascontiguousarray(ins["L0_ffn_W_fc"].T),
        wprojT=np.ascontiguousarray(ins["L0_ffn_W_proj"].T),
        aprojT=np.ascontiguousarray(ins["L0_ffn_A_proj"].T),
        bprojT=np.ascontiguousarray(ins["L0_ffn_B_proj"].T),
        alog=np.ascontiguousarray(ins["L0_attn_A_log"]),
        gam=np.ascontiguousarray(ins["L0_attn_gamma"][:, None]),
        bet=np.ascontiguousarray(ins["L0_attn_beta"][:, None]),
        dskip=np.ascontiguousarray(ins["L0_attn_D_skip"][:, None]),
        bd=np.ascontiguousarray(ins["L0_attn_b_delta"][:, None]),
        bbc=np.ascontiguousarray(
            np.concatenate([ins["L0_attn_b_B"], ins["L0_attn_b_C"]])[:, None]),
        bfc=np.ascontiguousarray(ins["L0_ffn_b_fc"][:, None]),
        bproj=np.ascontiguousarray(ins["L0_ffn_b_proj"][:, None]),
        ident=np.eye(128, dtype=np.float32),
    )

    nc = build_program()
    in_maps = []
    for b in range(n_cores):
        m = dict(common)
        m["x0"] = np.ascontiguousarray(x[b].T)
        in_maps.append(m)
    res = run_bass_kernel_spmd(nc, in_maps, core_ids=list(range(n_cores)))
    out = np.stack([np.asarray(r["out"]).T for r in res.results], axis=0)
    return np.ascontiguousarray(out.astype(np.float32))


# version 23
# speedup vs baseline: 1.3813x; 1.3813x over previous
import contextlib
import numpy as np

import concourse.bass as bass
import concourse.mybir as mybir
from concourse.bass_utils import run_bass_kernel_spmd

F32 = mybir.dt.float32
BF16 = mybir.dt.bfloat16
I32 = mybir.dt.int32
AF = mybir.ActivationFunctionType
ALU = mybir.AluOpType
AX = mybir.AxisListType

D = 1024
L = 2048
N = 16
FF = 4096
NB = D // 128
FB = FF // 128
TC = 512
NCH = L // TC
EPS = 1e-5
LORA_SCALE = 2.0
QMAX = 127.0
NDA = 8
NDBX = 4


def rep_row(src_ap, width):
    pitch = src_ap.ap[0][0]
    return bass.AP(src_ap.tensor, src_ap.offset, [[pitch, 1], [0, 128], [1, width]])


def build_program():
    nc = bass.Bass("TRN2", target_bir_lowering=False,
                   detect_race_conditions=False)

    def din(name, shape):
        return nc.dram_tensor(name, shape, F32, kind="ExternalInput").ap()

    x0_h = din("x0", [D, L])
    wd_h = din("wd", [D, D])
    wbc_h = din("wbc", [D, 32])
    afc_h = din("afc", [D, N])
    bfcT_h = din("bfcT", [N, FF])
    wfcT_h = din("wfcT", [D, FF])
    wprojT_h = din("wprojT", [FF, D])
    aprojT_h = din("aprojT", [FF, N])
    bprojT_h = din("bprojT", [N, D])
    alog_h = din("alog", [D, N])
    gam_h = din("gam", [D, 1])
    bet_h = din("bet", [D, 1])
    dskip_h = din("dskip", [D, 1])
    bd_h = din("bd", [D, 1])
    bbc_h = din("bbc", [32, 1])
    bfc_h = din("bfc", [FF, 1])
    bproj_h = din("bproj", [D, 1])
    ident_h = din("ident", [128, 128])
    sel2_h = din("sel2", [2, 256])
    repc_h = nc.dram_tensor("repc", [32, 32 * 128], BF16, kind="ExternalInput").ap()

    out_h = nc.dram_tensor("out", [D, L], F32, kind="ExternalOutput").ap()
    qfc_h = nc.dram_tensor("qfc_scr", [D, FF], BF16, kind="Internal").ap()
    qproj_h = nc.dram_tensor("qproj_scr", [FF, D], BF16, kind="Internal").ap()

    cnt = {}

    def inc(instr, sem, n=1):
        instr.then_inc(sem, n)
        cnt[sem.name] = cnt.get(sem.name, 0) + n
        return cnt[sem.name]

    def val(sem):
        return cnt.get(sem.name, 0)

    with contextlib.ExitStack() as ctx:
        E = ctx.enter_context
        sb = lambda name, shape, dt=F32: E(nc.sbuf_tensor(name, shape, dt))

        wdT = sb("s_wdT", [128, NB * D], BF16)
        wbc = sb("s_wbc", [128, NB * 32], BF16)
        afc = sb("s_afc", [128, NB * N], BF16)
        bfcT = sb("s_bfcT", [N, FF], BF16)
        aprojT = sb("s_aprojT", [128, FB * N], BF16)
        bprojT = sb("s_bprojT", [N, D], BF16)
        aneg = sb("s_aneg", [128, NB * N])
        gam = sb("s_gam", [128, NB])
        bet = sb("s_bet", [128, NB])
        dskip = sb("s_dskip", [128, NB])
        bd = sb("s_bd", [128, NB])
        bbc = sb("s_bbc", [32, 1])
        bfc = sb("s_bfc", [128, FB])
        bproj = sb("s_bproj", [128, NB])
        ones_bf = sb("s_ones", [128, 1], BF16)
        ident_bf = sb("s_ident", [128, 128], BF16)
        sel2 = sb("s_sel2", [2, 256])
        statAB = sb("s_statAB", [2, TC])
        statPQ = sb("s_statPQ", [2, TC])
        repc = sb("s_repc", [32, 32 * 128], BF16)
        onesf = sb("s_onesf", [128, 1])
        z1 = sb("s_z1", [128, 1])
        epsb = sb("s_epsb", [1, 1])
        amax_col = sb("s_amaxc", [128, 2])
        amax_rows = sb("s_amaxrows", [128, 40])
        qsc = sb("s_qsc", [1, 4])
        qsc_b = sb("s_qscb", [128, 4])
        stat = sb("s_stat", [1, 3 * TC])
        amax_row = stat[0:1, 0:256]
        bca = sb("s_bca", [128, 2 * TC])
        midpj = None
        bcm = sb("s_bcm", [32, L], BF16)
        midfc = sb("s_midfc", [N, L], BF16)
        carry = sb("s_carry", [128, NB * N])
        x2c = sb("s_x2c", [128, 2 * TC], BF16)
        x1f = sb("s_x1f", [128, NB * L], BF16)

        A1 = sb("s_arena1", [128, 20480])
        A2 = sb("s_arena2", [128, 2 * NB * TC])

        a1 = A1[:]
        alog = a1[:, 12288:12416]
        wstg = a1[:, 0:8192]
        amax_acc = a1[:, 8192:12288]
        qi32 = a1[:, 12288:16384].bitcast(I32)
        qbf = a1[:, 16384:20480].bitcast(BF16)
        brep = a1[:, 0:4096].bitcast(BF16)
        crep = a1[:, 4096:8192].bitcast(BF16)
        hbuf = a1[:, 8192:12288].bitcast(BF16)
        xn2c = a1[:, 12288:14336].bitcast(BF16)
        dltF = a1[:, 14336:16384].bitcast(BF16)
        uu = a1[:, 16384:16640].bitcast(BF16)
        tln = a1[:, 16640:17152]
        dbx2 = a1[:, 17152:18176].bitcast(BF16)
        da = a1[:, 18176:20224].bitcast(BF16)

        h1c = a1[:, 0:16384].bitcast(BF16)
        fcw = a1[:, 16384:20480].bitcast(BF16)

        a2 = A2[:]
        pjw = a2[:, 0:4096].bitcast(BF16)
        xnfc = a2[:, 4096:6144].bitcast(BF16)
        x2f = a2[:, 6144:7168]
        aptmp = a2[:, 7168:7680]
        stage = a2

        def x0t(slot, k):
            return a2[:, (slot * NB + k) * TC:(slot * NB + k + 1) * TC]

        pmm = [E(nc.psum_tensor("p_mm0", [128, 512], F32)),
               E(nc.psum_tensor("p_mm1", [128, 512], F32))]
        pmid = E(nc.psum_tensor("p_mid", [N, 512], F32))
        pln = E(nc.psum_tensor("p_ln", [1, 2 * TC], F32))
        pyc = E(nc.psum_tensor("p_yc", [128, 512], F32))
        prepA = E(nc.psum_tensor("p_repA", [128, 512], F32))
        prepB = E(nc.psum_tensor("p_repB", [128, 512], F32))

        qC = E(nc.semaphore("qC"))
        qW = [E(nc.semaphore("qW0")), E(nc.semaphore("qW1"))]
        qQ = [E(nc.semaphore("qQ0")), E(nc.semaphore("qQ1"))]
        qR = E(nc.semaphore("qR"))
        qX = [E(nc.semaphore("qX0")), E(nc.semaphore("qX1"))]
        qB = E(nc.semaphore("qB"))
        qFW = E(nc.semaphore("qFW"))
        qFW2 = E(nc.semaphore("qFW2"))
        qPW = E(nc.semaphore("qPW"))
        qPW2 = E(nc.semaphore("qPW2"))
        qO = [E(nc.semaphore("qO0")), E(nc.semaphore("qO1"))]
        sV = E(nc.semaphore("sV"))
        sY = E(nc.semaphore("sY"))
        sA = E(nc.semaphore("sA"))
        sP = E(nc.semaphore("sP"))
        sG = E(nc.semaphore("sG"))


        block = E(nc.Block())

        def SR(i):
            return stat[0:1, i * TC:(i + 1) * TC]

        @block.sync
        def _(sync):
            ctxnc = nc.allow_non_contiguous_dma(reason="tiny per-dblock vectors")
            ctxnc.__enter__()
            inc(sync.dma_start(out=stage[:, 0:NB * D].rearrange("p (b d) -> p b d", b=NB), in_=wd_h.rearrange("(b p) d -> p b d", p=128)), qC, 16)
            inc(sync.dma_start(out=alog[:].rearrange("p (b n) -> p b n", b=NB), in_=alog_h.rearrange("(b p) n -> p b n", p=128)), qC, 16)
            inc(sync.dma_start(out=gam[:].unsqueeze(2), in_=gam_h.rearrange("(b p) o -> p b o", p=128)), qC, 16)
            inc(sync.dma_start(out=bet[:].unsqueeze(2), in_=bet_h.rearrange("(b p) o -> p b o", p=128)), qC, 16)
            inc(sync.dma_start(out=dskip[:].unsqueeze(2), in_=dskip_h.rearrange("(b p) o -> p b o", p=128)), qC, 16)
            inc(sync.dma_start(out=bd[:].unsqueeze(2), in_=bd_h.rearrange("(b p) o -> p b o", p=128)), qC, 16)
            inc(sync.dma_start(out=bbc[:], in_=bbc_h), qC, 16)
            inc(sync.dma_start(out=bfc[:].unsqueeze(2), in_=bfc_h.rearrange("(b p) o -> p b o", p=128)), qC, 16)
            inc(sync.dma_start(out=bproj[:].unsqueeze(2), in_=bproj_h.rearrange("(b p) o -> p b o", p=128)), qC, 16)
            ctxnc.__exit__(None, None, None)
        v_c0 = val(qC)

        @block.vector
        def _(vector):
            vector.wait_ge(qC, v_c0)
            vector.memset(ones_bf[:], 1.0)
            vector.memset(onesf[:], 1.0)
            vector.memset(z1[:], 0.0)
            vector.memset(epsb[:], EPS)
            vector.memset(carry[:], 0.0)
            inc(vector.tensor_copy(out=wdT[:], in_=stage[:, 0:NB * D]), sV, 1)
        v_wdT = val(sV)

        @block.scalar
        def _(scalar):
            scalar.wait_ge(sV, v_wdT)
            inc(scalar.activation(out=stage[:, 0:NB * N], in_=alog[:], func=AF.Exp), sA, 1)
        v_aexp = val(sA)

        @block.vector
        def _(vector):
            vector.wait_ge(sA, v_aexp)
            inc(vector.tensor_scalar_mul(out=aneg[:], in0=stage[:, 0:NB * N], scalar1=-1.0), sV, 1)
        v_aneg = val(sV)

        @block.sync
        def _(sync):
            sync.wait_ge(sV, v_aneg)
            inc(sync.dma_start(out=stage[:, 0:NB * 32].rearrange("p (b d) -> p b d", b=NB), in_=wbc_h.rearrange("(b p) d -> p b d", p=128)), qC, 16)
            inc(sync.dma_start(out=stage[:, NB * 32:NB * 48].rearrange("p (b n) -> p b n", b=NB), in_=afc_h.rearrange("(b p) n -> p b n", p=128)), qC, 16)
            inc(sync.dma_start(out=stage[:, 512:512 + FB * N].rearrange("p (b n) -> p b n", b=FB), in_=aprojT_h.rearrange("(b p) n -> p b n", p=128)), qC, 16)
            inc(sync.dma_start(out=stage[0:N, 1024:1024 + FF], in_=bfcT_h), qC, 16)
            inc(sync.dma_start(out=stage[0:N, 5120:5120 + D], in_=bprojT_h), qC, 16)
            inc(sync.dma_start(out=stage[:, 6144:6144 + 128], in_=ident_h), qC, 16)
            inc(sync.dma_start(out=sel2[:], in_=sel2_h), qC, 16)
            inc(sync.dma_start(out=repc[:], in_=repc_h), qC, 16)
        v_c1 = val(qC)

        @block.vector
        def _(vector):
            vector.wait_ge(qC, v_c1)
            vector.tensor_copy(out=wbc[:], in_=stage[:, 0:NB * 32])
            vector.tensor_copy(out=afc[:], in_=stage[:, NB * 32:NB * 48])
            vector.tensor_copy(out=aprojT[:], in_=stage[:, 512:512 + FB * N])
            vector.tensor_scalar_mul(out=bfcT[:], in0=stage[0:N, 1024:1024 + FF], scalar1=LORA_SCALE)
            vector.tensor_scalar_mul(out=bprojT[:], in0=stage[0:N, 5120:5120 + D], scalar1=LORA_SCALE)
            inc(vector.tensor_copy(out=ident_bf[:], in_=stage[:, 6144:6144 + 128]), sV, 1)
        v_consts = val(sV)

        row_ms = []
        for wi, (w_h, klen, rows) in enumerate(((wfcT_h, FF, NB), (wprojT_h, D, FB))):
            for r in range(rows):
                g = len(row_ms)
                slot = g % 2

                @block.sync
                def _(sync, g=g, slot=slot, w_h=w_h, klen=klen, r=r):
                    if g >= 2:
                        sync.wait_ge(sV, row_ms[g - 2])
                    if g == 0:
                        sync.wait_ge(sV, v_consts)
                    inc(sync.dma_start(out=wstg[:, slot * FF:slot * FF + klen],
                                       in_=w_h[r * 128:(r + 1) * 128, :]), qW[slot], 16)
                v_ld = val(qW[slot])

                @block.vector
                def _(vector, g=g, slot=slot, klen=klen, v_ld=v_ld):
                    vector.wait_ge(qW[slot], v_ld)
                    row_ms.append(inc(vector.tensor_reduce(
                        out=amax_rows[:, g:g + 1],
                        in_=wstg[:, slot * FF:slot * FF + klen],
                        axis=AX.X, op=ALU.max, apply_absolute_value=True), sV, 1))

        @block.vector
        def _(vector):
            vector.wait_ge(sV, row_ms[-1])
            vector.tensor_reduce(out=amax_col[:, 0:1], in_=amax_rows[:, 0:NB],
                                 axis=AX.X, op=ALU.max)
            inc(vector.tensor_reduce(out=amax_col[:, 1:2], in_=amax_rows[:, NB:40],
                                     axis=AX.X, op=ALU.max), sV, 1)
        v_amax = val(sV)

        @block.sync
        def _(sync):
            sync.wait_ge(sV, v_amax)
            with nc.allow_non_contiguous_dma(reason="two 128-element transposes, once"):
                inc(sync.dma_start(out=amax_row[0:1, 0:128], in_=amax_col[:, 0:1]), qC, 16)
                inc(sync.dma_start(out=amax_row[0:1, 128:256], in_=amax_col[:, 1:2]), qC, 16)
        v_arow = val(qC)

        @block.vector
        def _(vector):
            vector.wait_ge(qC, v_arow)
            vector.tensor_reduce(out=qsc[0:1, 0:1], in_=amax_row[0:1, 0:128], axis=AX.X, op=ALU.max)
            vector.tensor_reduce(out=qsc[0:1, 2:3], in_=amax_row[0:1, 128:256], axis=AX.X, op=ALU.max)
            inc(vector.engine_nop(), sV, 1)
        v_maxv = val(sV)

        @block.vector
        def _(vector):
            vector.wait_ge(sV, v_maxv)
            vector.tensor_scalar_mul(out=qsc[0:1, 0:1], in0=qsc[0:1, 0:1], scalar1=1.0 / QMAX)
            inc(vector.tensor_scalar_mul(out=qsc[0:1, 2:3], in0=qsc[0:1, 2:3], scalar1=1.0 / QMAX), sV, 1)
        v_scale = val(sV)

        @block.vector
        def _(vector):
            vector.wait_ge(sV, v_scale)
            inc(vector.reciprocal(out=qsc[0:1, 1:2], in_=qsc[0:1, 0:1]), sV, 1)
            inc(vector.reciprocal(out=qsc[0:1, 3:4], in_=qsc[0:1, 2:3]), sV, 1)
        v_inv = val(sV)

        @block.sync
        def _(sync):
            sync.wait_ge(sV, v_inv)
            inc(sync.dma_start(out=qsc_b[:].unsqueeze(1), in_=rep_row(qsc[0:1, 0:4], 4)), qC, 16)
        v_qscb = val(qC)

        q_dve = []
        q_st = []
        for wi, (w_h, q_h, klen, rows) in enumerate(
                ((wfcT_h, qfc_h, FF, NB), (wprojT_h, qproj_h, D, FB))):
            inv_ap = qsc_b[:, 2 * wi + 1:2 * wi + 2]
            sc_ap = qsc_b[:, 2 * wi:2 * wi + 1]
            for r in range(rows):
                g = len(q_dve)
                slot = g % 2

                @block.sync
                def _(sync, g=g, slot=slot, w_h=w_h, klen=klen, r=r):
                    if g >= 2:
                        sync.wait_ge(sV, q_dve[g - 2])
                    else:
                        sync.wait_ge(sV, row_ms[-1])
                    inc(sync.dma_start(out=wstg[:, slot * FF:slot * FF + klen],
                                       in_=w_h[r * 128:(r + 1) * 128, :]), qW[slot], 16)
                v_ld = val(qW[slot])

                @block.vector
                def _(vector, g=g, slot=slot, klen=klen, v_ld=v_ld,
                      inv_ap=inv_ap, sc_ap=sc_ap):
                    vector.wait_ge(qW[slot], v_ld)
                    vector.wait_ge(qC, v_qscb)
                    if g >= 2:
                        vector.wait_ge(qQ[slot], q_st[g - 2])
                    w = wstg[:, slot * FF:slot * FF + klen]
                    vector.tensor_scalar(out=qi32[:, 0:klen], in0=w, scalar1=inv_ap,
                                         scalar2=None, op0=ALU.mult)
                    q_dve.append(inc(vector.tensor_scalar(
                        out=qbf[:, slot * FF:slot * FF + klen], in0=qi32[:, 0:klen],
                        scalar1=sc_ap, scalar2=None, op0=ALU.mult), sV, 1))

                @block.sync
                def _(sync, g=g, slot=slot, q_h=q_h, klen=klen, r=r):
                    sync.wait_ge(sV, q_dve[g])
                    q_st.append(inc(sync.dma_start(
                        out=q_h[r * 128:(r + 1) * 128, :],
                        in_=qbf[:, slot * FF:slot * FF + klen]), qQ[slot], 16))
        v_quant_q0 = val(qQ[0])
        v_quant_q1 = val(qQ[1])
        v_quant_v = val(sV)

        midpj = bca[:].bitcast(BF16)[0:N, 0:TC]
        P1B = bca[:, 0:TC]
        Q1B = bca[:, TC:2 * TC]
        CB = bca[:, 0:TC]
        ALB = bca[:, TC:2 * TC]

        v_x1_done = []
        v_x1_list = []
        v_yred_list = []
        v_yredseg_list = []
        v_exp_list = []
        v_u_list = []
        v_ldx_by_chunk = []
        sums_base = {}
        v_m1_by_chunk = {}
        v_scan_last = 0
        mA_sq_saved = None
        prev_prep_guards = None
        prev_bcm_mid = 0

        def emit_x0_load(c):
            slot = c % 2

            @block.sync
            def _(sync, c=c, slot=slot):
                if c >= 2:
                    sync.wait_ge(sV, v_x1_done[c - 2])
                dst = a2[:, slot * NB * TC:(slot + 1) * NB * TC].rearrange(
                    "p (k t) -> p k t", k=NB)
                srcv = x0_h.rearrange("(k p) t -> p k t", p=128)[:, :, c * TC:(c + 1) * TC]
                inc(sync.dma_start(out=dst, in_=srcv), qX[slot], 16)
            v_ldx_by_chunk.append(val(qX[slot]))

        emit_x0_load(0)
        emit_x0_load(1)

        for c in range(NCH):
            slot = c % 2
            v_scan_last_entry = v_scan_last

            if c == 0:
                P0 = val(sA)
                mA_sq = [P0 + 1 + k for k in range(NB)]
                pe_base0 = val(sP)

                @block.scalar
                def _(scalar, pe_base0=pe_base0):
                    scalar.wait_ge(qX[0], v_ldx_by_chunk[0])
                    scalar.wait_ge(sV, v_quant_v)
                    scalar.wait_ge(qQ[0], v_quant_q0)
                    scalar.wait_ge(qQ[1], v_quant_q1)
                    for k in range(NB):
                        if k >= 2:
                            scalar.wait_ge(sP, pe_base0 + k - 1)
                        inc(scalar.activation(
                            out=x2c[:, (k % 2) * TC:(k % 2 + 1) * TC],
                            in_=x0t(0, k), func=AF.Square), sA, 1)

                sums_base[0] = val(sP)

                @block.tensor
                def _(tensor, mA_sq=mA_sq):
                    tensor.wait_ge(sV, v_consts)
                    for k in range(NB):
                        tensor.wait_ge(sA, mA_sq[k])
                        tensor.matmul(out=pln[0:1, 0:TC], lhsT=onesf[:],
                                      rhs=x0t(0, k),
                                      start=(k == 0), stop=(k == NB - 1))
                        inc(tensor.matmul(out=pln[0:1, TC:2 * TC], lhsT=ones_bf[:],
                                          rhs=x2c[:, (k % 2) * TC:(k % 2 + 1) * TC],
                                          start=(k == 0), stop=(k == NB - 1)), sP, 1)
            else:
                mA_sq = mA_sq_saved

            A0 = val(sA)
            nsq = NB if c + 1 < NCH else 0
            order = (['sqrt', 'exp0', 'ln0', 'bcm', 'mid']
                     + ['ev%d' % j for j in range(4)]
                     + ['da_0_%d' % n for n in range(8)]
                     + ['ev%d' % j for j in range(4, 16)]
                     + ['p1ev', 'q1ev']
                     + ['ev%d' % j for j in range(16, 32)]
                     + ['da_0_%d' % n for n in range(8, 16)])
            order += ['sq%d' % j for j in range(nsq)]
            for k in range(1, NB):
                order += ['exp%d' % k, 'ln%d' % k]
                order += ['da_%d_%d' % (k, n) for n in range(N)]
            mA = {nm: A0 + 1 + i for i, nm in enumerate(order)}
            A_end = A0 + len(order)
            mA_sqrt = mA['sqrt']
            mA_exp = [mA['exp%d' % k] for k in range(NB)]
            mA_ln = [mA['ln%d' % k] for k in range(NB)]
            mA_da = [[mA['da_%d_%d' % (k, n)] for n in range(N)] for k in range(NB)]
            m_evac = [mA['ev%d' % j] for j in range(32)]
            m_p1ev, m_q1ev = mA['p1ev'], mA['q1ev']
            mA_sq_next = [mA['sq%d' % j] for j in range(nsq)]
            mA_sq_saved = mA_sq_next

            v_sums = sums_base[c] + NB

            @block.vector
            def _(vector, v_sums=v_sums):
                vector.wait_ge(sP, v_sums)
                vector.tensor_scalar_mul(out=SR(0), in0=pln[0:1, 0:TC], scalar1=1.0 / D)
                inc(vector.tensor_scalar_mul(out=SR(1), in0=pln[0:1, TC:2 * TC],
                                             scalar1=1.0 / D), sV, 1)
            v_m1 = val(sV)
            v_m1_by_chunk[c] = v_m1

            @block.vector
            def _(vector, v_m1=v_m1):
                vector.tensor_mul(out=SR(2), in0=SR(0), in1=SR(0))
                vector.tensor_sub(out=SR(1), in0=SR(1), in1=SR(2))
                vector.tensor_scalar(out=SR(2), in0=SR(1), scalar1=1.0 + EPS,
                                     scalar2=EPS * EPS, op0=ALU.mult,
                                     op1=ALU.add)
                inc(vector.tensor_scalar_add(out=SR(1), in0=SR(1),
                                             scalar1=EPS), sV, 1)
            v_v1 = val(sV)

            @block.vector
            def _(vector, mA_sqrt=mA_sqrt):
                vector.wait_ge(sA, mA_sqrt)
                inc(vector.reciprocal(out=stat[0:1, 1 * TC:3 * TC],
                                      in_=stat[0:1, 1 * TC:3 * TC]), sV, 1)
            v_rv = val(sV)

            @block.sync
            def _(sync, v_rv=v_rv):
                sync.wait_ge(sV, v_rv)
                inc(sync.dma_start(out=statAB[0:1, :], in_=SR(0)), qB, 16)
                inc(sync.dma_start(out=statAB[1:2, :], in_=SR(2)), qB, 16)
            v_ab = val(qB)

            @block.vector
            def _(vector, v_ab=v_ab):
                vector.wait_ge(qB, v_ab)
                vector.tensor_mul(out=SR(2), in0=SR(0), in1=SR(1))
                inc(vector.tensor_scalar_add(out=SR(1), in0=SR(1),
                                             scalar1=1.0), sV, 1)
            v_stats = val(sV)

            @block.sync
            def _(sync, v_stats=v_stats):
                sync.wait_ge(sV, v_stats)
                inc(sync.dma_start(out=statPQ[0:1, :], in_=SR(1)), qB, 16)
                inc(sync.dma_start(out=statPQ[1:2, :], in_=SR(2)), qB, 16)
            v_pq = val(qB)

            @block.tensor
            def _(tensor, c=c, v_ab=v_ab, pg=prev_prep_guards):
                tensor.wait_ge(qB, v_ab)
                if pg is not None:
                    tensor.wait_ge(sA, pg[1])
                tensor.matmul(out=prepA[:, 0:TC], lhsT=sel2[:, 0:128], rhs=statAB[:],
                              start=True, stop=True)
                inc(tensor.matmul(out=prepB[:, 0:TC], lhsT=sel2[:, 128:256], rhs=statAB[:],
                                  start=True, stop=True), sP, 1)
            v_cbmm = val(sP)

            v_xn2pre = []

            @block.vector
            def _(vector, c=c, slot=slot, v_cbmm=v_cbmm):
                vector.wait_ge(sP, v_cbmm)
                if c == 0:
                    vector.wait_ge(qQ[0], v_quant_q0)
                    vector.wait_ge(qQ[1], v_quant_q1)
                else:
                    vector.wait_ge(sV, v_x1_done[c - 1])
                for k in range(NB):
                    vector.tensor_sub(out=tln, in0=x0t(slot, k), in1=prepA[:, 0:TC])
                    vector.scalar_tensor_tensor(
                        out=tln, in0=tln, scalar=gam[:, k:k + 1], in1=prepB[:, 0:TC],
                        op0=ALU.mult, op1=ALU.mult)
                    v_xn2pre.append(inc(vector.tensor_scalar(
                        out=xn2c[:, k * TC:(k + 1) * TC], in0=tln,
                        scalar1=bet[:, k:k + 1], scalar2=None,
                        op0=ALU.add), sV, 1))

            gk0 = c * NB
            v_exp_list.append(mA_exp[0])

            @block.tensor
            def _(tensor, c=c, xn2pre=v_xn2pre, pbm=prev_bcm_mid, gk0=gk0):
                if pbm:
                    tensor.wait_ge(sA, pbm)
                if gk0 >= 2:
                    tensor.wait_ge(sA, v_exp_list[gk0 - 2])
                if gk0 >= 1:
                    tensor.wait_ge(sA, v_exp_list[gk0 - 1])
                for k in range(NB):
                    tensor.wait_ge(sV, xn2pre[k])
                    tensor.matmul(out=pmm[1][0:32, 0:TC], lhsT=wbc[:, k * 32:(k + 1) * 32],
                                  rhs=xn2c[:, k * TC:(k + 1) * TC],
                                  start=(k == 0), stop=(k == NB - 1))
                    tensor.matmul(out=pmid[:, 0:TC], lhsT=afc[:, k * N:(k + 1) * N],
                                  rhs=xn2c[:, k * TC:(k + 1) * TC],
                                  start=(k == 0), stop=(k == NB - 1))
                    ins = tensor.matmul(
                        out=pmm[0][:, 0:TC],
                        lhsT=wdT[:, k * D + 0 * 128:k * D + 1 * 128],
                        rhs=xn2c[:, k * TC:(k + 1) * TC],
                        start=(k == 0), stop=(k == NB - 1))
                    if k == NB - 1:
                        inc(ins, sP, 1)
            v_bcmm = val(sP)

            rep_sp = {}

            @block.tensor
            def _(tensor, c=c, mA=mA, m_evac=m_evac, xn2pre=v_xn2pre,
                  v_scans_prev=(v_u_list[-1] + N if c >= 1 else 0),
                  v_ym_prev=v_scan_last_entry):
                tensor.wait_ge(sA, mA['bcm'])
                if c == 0:
                    tensor.wait_ge(sV, v_quant_v)
                else:
                    tensor.wait_ge(sV, v_scans_prev)
                    tensor.wait_ge(sY, v_ym_prev)
                for j in range(16):
                    if j < 2:
                        tensor.wait_ge(sV, xn2pre[NB - 1])
                    else:
                        tensor.wait_ge(sA, m_evac[j - 2])
                    rep_sp[j] = inc(tensor.matmul(
                        out=(prepA if j % 2 == 0 else prepB)[:, 0:TC],
                        lhsT=repc[:, j * 128:(j + 1) * 128],
                        rhs=bcm[:, c * TC:(c + 1) * TC],
                        start=True, stop=True), sP, 1)

            @block.tensor
            def _(tensor, m_evac=m_evac, v_pq=v_pq):
                tensor.wait_ge(qB, v_pq)
                tensor.wait_ge(sA, m_evac[14])
                sp1 = inc(tensor.matmul(out=prepA[:, 0:TC], lhsT=sel2[:, 0:128],
                                        rhs=statPQ[:], start=True, stop=True), sP, 1)
                tensor.wait_ge(sA, m_evac[15])
                sq1 = inc(tensor.matmul(out=prepB[:, 0:TC], lhsT=sel2[:, 128:256],
                                        rhs=statPQ[:], start=True, stop=True), sP, 1)
                rep_sp['p1'] = sp1
                rep_sp['q1'] = sq1

            @block.tensor
            def _(tensor, c=c, mA=mA, m_evac=m_evac):
                for j in range(16, 32):
                    if j == 16:
                        tensor.wait_ge(sA, mA['p1ev'])
                    elif j == 17:
                        tensor.wait_ge(sA, mA['q1ev'])
                    else:
                        tensor.wait_ge(sA, m_evac[j - 2])
                    rep_sp[j] = inc(tensor.matmul(
                        out=(prepA if j % 2 == 0 else prepB)[:, 0:TC],
                        lhsT=repc[:, j * 128:(j + 1) * 128],
                        rhs=bcm[:, c * TC:(c + 1) * TC],
                        start=True, stop=True), sP, 1)
            prev_prep_guards = (m_evac[30], m_evac[31])
            prev_bcm_mid = mA['mid']

            if c + 1 < NCH:
                sums_base[c + 1] = val(sP)

                @block.tensor
                def _(tensor, c=c, mA_sq_next=mA_sq_next, v_m1=v_m1,
                      nslot=(c + 1) % 2):
                    tensor.wait_ge(sV, v_m1)
                    for k in range(NB):
                        tensor.wait_ge(sA, mA_sq_next[k])
                        tensor.matmul(out=pln[0:1, 0:TC], lhsT=onesf[:],
                                      rhs=x0t(nslot, k),
                                      start=(k == 0), stop=(k == NB - 1))
                        inc(tensor.matmul(out=pln[0:1, TC:2 * TC], lhsT=ones_bf[:],
                                          rhs=x2c[:, (k % 2) * TC:(k % 2 + 1) * TC],
                                          start=(k == 0), stop=(k == NB - 1)), sP, 1)

            v_dmm_list = [v_bcmm]

            def emit_delta(k, c=c):
                gk = c * NB + k
                mmslot = k % 2

                @block.tensor
                def _(tensor, c=c, k=k, gk=gk, mmslot=mmslot, mA=mA):
                    if k == 1:
                        tensor.wait_ge(sA, mA['bcm'])
                    if gk >= 2:
                        tensor.wait_ge(sA, v_exp_list[gk - 2])
                    for kk in range(NB):
                        ins = tensor.matmul(
                            out=pmm[mmslot][:, 0:TC],
                            lhsT=wdT[:, kk * D + k * 128:kk * D + (k + 1) * 128],
                            rhs=xn2c[:, kk * TC:(kk + 1) * TC],
                            start=(kk == 0), stop=(kk == NB - 1))
                        if kk == NB - 1:
                            inc(ins, sP, 1)
                v_dmm_list.append(val(sP))
                v_exp_list.append(mA_exp[k])

            for k in range(NB):
                gk = c * NB + k

                @block.vector
                def _(vector, c=c, k=k, gk=gk, mA_ln=mA_ln):
                    vector.wait_ge(sA, mA_ln[k])
                    v_u_list.append(inc(vector.tensor_tensor(
                        out=uu, in0=dltF[:, k * TC:(k + 1) * TC],
                        in1=xn2c[:, k * TC:(k + 1) * TC],
                        op=ALU.mult), sV, 1))
                v_u = v_u_list[-1]

                v_dbx_list_k = []
                carry_ms = {}

                @block.gpsimd
                def _(gpsimd, c=c, k=k, gk=gk, v_u=v_u, m_evac=m_evac,
                      carry_ms=carry_ms):
                    h3 = hbuf.rearrange("p (n t) -> p n t", n=N)
                    gpsimd.wait_ge(sV, v_u)
                    for n in range(N):
                        if k == 0:
                            gpsimd.wait_ge(sA, m_evac[2 * n])
                        if n >= NDBX:
                            gpsimd.wait_ge(sV, v_u + n - NDBX + 1)
                        elif gk >= 1:
                            gpsimd.wait_ge(sV, v_u_list[gk - 1] + 17 + n - NDBX)
                        v_dbx_list_k.append(inc(gpsimd.tensor_tensor(
                            out=dbx2[:, (n % NDBX) * TC:(n % NDBX + 1) * TC],
                            in0=uu, in1=brep[:, n * TC:(n + 1) * TC],
                            op=ALU.mult), sG, 1))
                        if n in (5, 9, 13, 15):
                            q = {5: 0, 9: 1, 13: 2, 15: 3}[n]
                            gpsimd.wait_ge(sV, v_u + 4 * q + 4)
                            carry_ms[q] = inc(gpsimd.tensor_tensor(
                                out=carry[:, k * N + q * 4:k * N + q * 4 + 4],
                                in0=h3[:, q * 4:q * 4 + 4, TC - 1:TC].squeeze(2),
                                in1=z1[:].broadcast_to((128, 4)),
                                op=ALU.add), sG, 1)

                v_ymq = []

                @block.vector
                def _(vector, c=c, k=k, gk=gk, mA_da=mA_da, dbxms=v_dbx_list_k,
                      m_evac=m_evac, carry_ms=carry_ms):
                    for n in range(N):
                        vector.wait_ge(sA, mA_da[k][n])
                        vector.wait_ge(sG, dbxms[n])
                        if gk >= 1:
                            vector.wait_ge(sP, v_yredseg_list[gk - 1][n // 4])
                        inc(vector.tensor_tensor_scan(
                            out=hbuf[:, n * TC:(n + 1) * TC],
                            data0=da[:, (n % NDA) * TC:(n % NDA + 1) * TC],
                            data1=dbx2[:, (n % NDBX) * TC:(n % NDBX + 1) * TC],
                            initial=carry[:, k * N + n:k * N + n + 1],
                            op0=ALU.mult, op1=ALU.add), sV, 1)
                        if n % 4 == 3:
                            q = n // 4
                            if k == 0:
                                vector.wait_ge(sA, m_evac[8 * q + 7])
                            vector.wait_ge(sG, carry_ms[q])
                            v_ymq.append(inc(vector.tensor_tensor(
                                out=hbuf[:, q * 4 * TC:(q + 1) * 4 * TC],
                                in0=hbuf[:, q * 4 * TC:(q + 1) * 4 * TC],
                                in1=crep[:, q * 4 * TC:(q + 1) * 4 * TC],
                                op=ALU.mult), sY, 1))
                v_y = v_ymq[-1]
                v_scan_last = v_y

                if k + 1 < NB:
                    emit_delta(k + 1)

                yseg = []

                @block.tensor
                def _(tensor, c=c, k=k, gk=gk, ymq=v_ymq):
                    if gk >= 1:
                        tensor.wait_ge(sV, v_x1_list[gk - 1])
                    for q in range(4):
                        tensor.wait_ge(sY, ymq[q])
                        for n in range(q * 4, q * 4 + 4):
                            ins = tensor.matmul(out=pyc[:, 0:TC], lhsT=ident_bf[:],
                                                rhs=hbuf[:, n * TC:(n + 1) * TC],
                                                start=(n == 0), stop=(n == N - 1))
                        yseg.append(inc(ins, sP, 1))
                v_yredseg_list.append(yseg)
                v_yred_list.append(yseg[-1])
                v_yred = v_yred_list[-1]

                @block.vector
                def _(vector, c=c, k=k, slot=slot, v_yred=v_yred,
                      m_q1ev=m_q1ev):
                    if k == 0:
                        vector.wait_ge(sA, m_q1ev)
                    vector.wait_ge(sP, v_yred)
                    x1s = x1f[:, k * L + c * TC:k * L + (c + 1) * TC]
                    vector.tensor_tensor(out=tln, in0=x0t(slot, k), in1=P1B, op=ALU.mult)
                    vector.tensor_sub(out=tln, in0=tln, in1=Q1B)
                    vector.tensor_add(out=tln, in0=tln, in1=pyc[:, 0:TC])
                    v_x1_list.append(inc(vector.scalar_tensor_tensor(
                        out=x1s, in0=xn2c[:, k * TC:(k + 1) * TC],
                        scalar=dskip[:, k:k + 1], in1=tln,
                        op0=ALU.mult, op1=ALU.add), sV, 1))
            v_x1_done.append(v_x1_list[-1])
            if c + 2 < NCH:
                emit_x0_load(c + 2)

            sq_guard_lo = sums_base[c] + NB - 1
            sq_guard_next = sums_base.get(c + 1)

            @block.scalar
            def _(scalar, c=c, slot=slot, mA=mA, rep_sp=rep_sp,
                  v_v1=v_v1, xn2pre=v_xn2pre, v_bcmm=v_bcmm, dmms=v_dmm_list,
                  nsq=nsq, order=order,
                  sq_guard_lo=sq_guard_lo, sq_guard_next=sq_guard_next):
                scalar.wait_ge(sV, v_v1)
                scalar.activation(out=SR(1), in_=SR(1), func=AF.Sqrt)
                inc(scalar.activation(out=SR(2), in_=SR(2), func=AF.Sqrt), sA, 1)
                for nm in order[1:]:
                    if nm == 'exp0' or nm.startswith('exp'):
                        k = int(nm[3:])
                        scalar.wait_ge(sP, dmms[k])
                        inc(scalar.activation(
                            out=dltF[:, k * TC:(k + 1) * TC],
                            in_=pmm[k % 2][:, 0:TC],
                            func=AF.Exp, bias=bd[:, k:k + 1]), sA, 1)
                    elif nm.startswith('ln'):
                        k = int(nm[2:])
                        inc(scalar.activation(
                            out=dltF[:, k * TC:(k + 1) * TC],
                            in_=dltF[:, k * TC:(k + 1) * TC],
                            func=AF.Ln, bias=onesf[:]), sA, 1)
                    elif nm == 'bcm':
                        scalar.wait_ge(sP, v_bcmm)
                        inc(scalar.activation(out=bcm[:, c * TC:(c + 1) * TC],
                                              in_=pmm[1][0:32, 0:TC], func=AF.Identity,
                                              bias=bbc[:]), sA, 1)
                    elif nm == 'mid':
                        inc(scalar.activation(out=midfc[:, c * TC:(c + 1) * TC],
                                              in_=pmid[:, 0:TC], func=AF.Identity,
                                              scale=LORA_SCALE), sA, 1)
                    elif nm.startswith('ev'):
                        j = int(nm[2:])
                        scalar.wait_ge(sP, rep_sp[j])
                        dstb = brep if j % 2 == 0 else crep
                        inc(scalar.activation(
                            out=dstb[:, (j // 2) * TC:(j // 2 + 1) * TC],
                            in_=(prepA if j % 2 == 0 else prepB)[:, 0:TC],
                            func=AF.Identity), sA, 1)
                    elif nm == 'p1ev':
                        if c >= 1:
                            scalar.wait_ge(sV, v_x1_done[c - 1])
                        scalar.wait_ge(sP, rep_sp['p1'])
                        inc(scalar.activation(out=bca[:, 0:TC], in_=prepA[:, 0:TC],
                                              func=AF.Identity), sA, 1)
                    elif nm == 'q1ev':
                        scalar.wait_ge(sP, rep_sp['q1'])
                        inc(scalar.activation(out=bca[:, TC:2 * TC], in_=prepB[:, 0:TC],
                                              func=AF.Identity), sA, 1)
                    elif nm.startswith('sq'):
                        j = int(nm[2:])
                        if j == 0:
                            scalar.wait_ge(qX[(slot + 1) % 2], v_ldx_by_chunk[c + 1])
                        if j >= 2:
                            scalar.wait_ge(sP, sq_guard_next + j - 1)
                        else:
                            scalar.wait_ge(sP, sq_guard_lo + j)
                        inc(scalar.activation(
                            out=x2c[:, (j % 2) * TC:(j % 2 + 1) * TC],
                            in_=x0t((slot + 1) % 2, j),
                            func=AF.Square), sA, 1)
                    elif nm.startswith('da_'):
                        k, n = (int(t) for t in nm[3:].split('_'))
                        gk = c * NB + k
                        if n >= NDA:
                            scalar.wait_ge(sV, v_u_list[gk] + n - NDA + 1)
                        elif gk >= 1:
                            scalar.wait_ge(sV, v_u_list[gk - 1] + 17 + n - NDA)
                        inc(scalar.activation(
                            out=da[:, (n % NDA) * TC:(n % NDA + 1) * TC],
                            in_=dltF[:, k * TC:(k + 1) * TC], func=AF.Exp,
                            scale=aneg[:, k * N + n:k * N + n + 1]), sA, 1)
                    else:
                        raise AssertionError(nm)
            assert val(sA) == A_end, (c, val(sA), A_end)
        v_attn = val(sV)
        v_m1_prev = v_m1_by_chunk[NCH - 1]

        fc_grp_ms = []
        pj_ms = []
        x2a_all = []
        x2st_all = []
        v_bcev_last = prev_bcm_mid
        HS = FB * TC

        def h1s(s):
            return h1c[:, s * HS:(s + 1) * HS]

        mPre_base = val(sA)

        @block.scalar
        def _(scalar):
            scalar.wait_ge(sV, v_attn)
            for k in range(NB):
                inc(scalar.activation(
                    out=h1s(0)[:, k * TC:(k + 1) * TC],
                    in_=x1f[:, k * L:k * L + TC],
                    func=AF.Square), sA, 1)
        mPre_sq = [mPre_base + 1 + k for k in range(NB)]

        sums_pe = {}

        @block.tensor
        def _(tensor):
            tensor.wait_ge(sV, v_m1_prev)
            for k in range(NB):
                tensor.wait_ge(sA, mPre_sq[k])
                tensor.matmul(out=pln[0:1, 0:TC], lhsT=ones_bf[:],
                              rhs=x1f[:, k * L:k * L + TC],
                              start=(k == 0), stop=(k == NB - 1))
                inc(tensor.matmul(out=pln[0:1, TC:2 * TC], lhsT=ones_bf[:],
                                  rhs=h1s(0)[:, k * TC:(k + 1) * TC],
                                  start=(k == 0), stop=(k == NB - 1)), sP, 1)
        sums_pe[0] = val(sP)

        def emit_fstats_a(cc):
            @block.vector
            def _(vector, v_s=sums_pe[cc]):
                vector.wait_ge(sP, v_s)
                vector.tensor_scalar_mul(out=SR(0), in0=pln[0:1, 0:TC], scalar1=1.0 / D)
                vector.tensor_scalar_mul(out=SR(1), in0=pln[0:1, TC:2 * TC], scalar1=1.0 / D)
                vector.tensor_mul(out=SR(2), in0=SR(0), in1=SR(0))
                vector.tensor_sub(out=SR(1), in0=SR(1), in1=SR(2))
                inc(vector.tensor_scalar_add(out=SR(1), in0=SR(1), scalar1=EPS), sV, 1)
            return val(sV)

        def emit_fstats_b(sqrt_ms):
            @block.vector
            def _(vector, sqrt_ms=sqrt_ms):
                vector.wait_ge(sA, sqrt_ms)
                vector.reciprocal(out=SR(1), in_=SR(1))
                inc(vector.tensor_mul(out=SR(2), in0=SR(0), in1=SR(1)),
                    sV, 1)
            v_sb = val(sV)

            @block.sync
            def _(sync, v_sb=v_sb):
                sync.wait_ge(sV, v_sb)
                inc(sync.dma_start(out=statAB[0:1, :], in_=SR(1)), qB, 16)
                inc(sync.dma_start(out=statAB[1:2, :], in_=SR(2)), qB, 16)
            return val(qB)

        def emit_fouter(v_sf, bank_guard_sem, bank_guard):
            @block.tensor
            def _(tensor, v_sf=v_sf, bg=bank_guard):
                tensor.wait_ge(qB, v_sf)
                tensor.wait_ge(bank_guard_sem, bg)
                tensor.matmul(out=prepA[:, 0:TC], lhsT=sel2[:, 0:128], rhs=statAB[:],
                              start=True, stop=True)
                inc(tensor.matmul(out=prepB[:, 0:TC], lhsT=sel2[:, 128:256], rhs=statAB[:],
                                  start=True, stop=True), sP, 1)
            return val(sP)

        v_xnf_list = {}

        def emit_fapply(cc, v_outer, v_fc_guard):
            @block.vector
            def _(vector, cc=cc, v_outer=v_outer, v_fc=v_fc_guard):
                vector.wait_ge(sP, v_outer)
                if v_fc:
                    vector.wait_ge(sP, v_fc)
                for k in range(NB):
                    vector.tensor_tensor(
                        out=aptmp, in0=x1f[:, k * L + cc * TC:k * L + (cc + 1) * TC],
                        in1=prepA[:, 0:TC], op=ALU.mult)
                    inc(vector.tensor_sub(
                        out=xnfc[:, k * TC:(k + 1) * TC], in0=aptmp,
                        in1=prepB[:, 0:TC]), sV, 1)
            v_xnf_list[cc] = val(sV)

        def fcw_dst_g(wslot, lo, hi):
            return fcw[:, wslot * NB * 512:(wslot + 1) * NB * 512].rearrange(
                "p (k t) -> p k t", k=NB)[:, lo:hi]

        def fcw_src_g(g, lo, hi):
            return qfc_h.rearrange("(k p) f -> p k f", p=128)[
                :, lo:hi, g * 512:(g + 1) * 512]

        def pjw_dst_k(wslot, lo, hi):
            return pjw[:, wslot * FB * 128:(wslot + 1) * FB * 128].rearrange(
                "p (a d) -> p a d", a=FB)[:, lo:hi]

        def pjw_src_k(k, lo, hi):
            return qproj_h[:, k * 128:(k + 1) * 128].rearrange(
                "(a p) d -> p a d", p=128)[:, lo:hi]

        def qfw2_cnt(c, g):
            if c == 0 and g <= 1:
                return 16 * (g + 1)
            if g >= 2:
                return 16 * (3 + 8 * c + g - 2)
            return 16 * (9 + 8 * (c - 1) + g)

        qpw2_cnt = qfw2_cnt

        @block.scalar
        def _(scalar):
            scalar.wait_ge(sV, v_attn)
            scalar.wait_ge(qQ[0], v_quant_q0)
            scalar.wait_ge(qQ[1], v_quant_q1)
            for g in (0, 1):
                inc(scalar.dma_start(out=fcw_dst_g(g % 2, 4, 8),
                                     in_=fcw_src_g(g, 4, 8)), qFW2, 16)
            for k in (0, 1):
                inc(scalar.dma_start(out=pjw_dst_k(k % 2, 16, 32),
                                     in_=pjw_src_k(k, 16, 32)), qPW2, 16)

        v_vf0 = emit_fstats_a(0)
        pre_sqrt_base = val(sA)

        @block.scalar
        def _(scalar, v_vf0=v_vf0):
            scalar.wait_ge(sV, v_vf0)
            inc(scalar.activation(out=SR(1), in_=SR(1), func=AF.Sqrt), sA, 1)
        v_sf0 = emit_fstats_b(pre_sqrt_base + 1)
        v_out0mm = emit_fouter(v_sf0, sA, prev_prep_guards[1])
        emit_fapply(0, v_out0mm, 0)

        v_midp_prev = 0
        v_statsF_prev = v_sf0
        for c in range(NCH):
            pipe = c + 1 < NCH

            F0 = val(sA)
            order_f = []
            for f in range(8):
                order_f += ['gelu%d' % f]
                if f == 3:
                    order_f += ['ldf2']
                if f == 7:
                    order_f += ['ldf3']
            if pipe:
                order_f += ['fsq%d' % j for j in range(NB)]
            for f in range(8, FB):
                order_f += ['gelu%d' % f]
                if f % 4 == 3 and 2 + (f - 3) // 4 <= 7:
                    order_f += ['ldf%d' % (2 + (f - 3) // 4)]
                if f == 27 and pipe:
                    order_f += ['ldf8']
                if f == 31 and pipe:
                    order_f += ['ldf9']
            if pipe:
                order_f += ['fsqrt']
            for k in range(NB):
                order_f += ['x2a%d' % k]
                if k + 2 < NB:
                    order_f += ['ldp%d' % (k + 2)]
                elif pipe:
                    order_f += ['ldp%d' % (k + 2)]
            acts_only = [nm for nm in order_f
                         if not (nm.startswith('ldf') or nm.startswith('ldp'))]
            mF = {nm: F0 + 1 + i for i, nm in enumerate(acts_only)}
            F_end = F0 + len(acts_only)
            mF_gelu = [mF['gelu%d' % f] for f in range(FB)]
            mF_x2a = [mF['x2a%d' % k] for k in range(NB)]

            v_pe_f_base = val(sP)

            fcw_ld = []
            for g in range(FB // 4):
                wslot = g % 2
                guard = fc_grp_ms[-2] if len(fc_grp_ms) >= 2 else None

                @block.sync
                def _(sync, c=c, g=g, guard=guard, wslot=wslot):
                    if c == 0 and g <= 1:
                        sync.wait_ge(sV, v_attn)
                        sync.wait_ge(qQ[0], v_quant_q0)
                        sync.wait_ge(qQ[1], v_quant_q1)
                    if guard is not None:
                        sync.wait_ge(sP, guard)
                    inc(sync.dma_start(out=fcw_dst_g(wslot, 0, 4),
                                       in_=fcw_src_g(g, 0, 4)), qFW, 16)
                fcw_ld.append(val(qFW))

                for fi in range(4):
                    f = g * 4 + fi
                    bank = f % 2

                    @block.tensor
                    def _(tensor, c=c, g=g, f=f, fi=fi, wslot=wslot, bank=bank,
                          mF_gelu=mF_gelu,
                          x2a_prev=(x2a_all[-1] if x2a_all else None)):
                        tensor.wait_ge(qFW, fcw_ld[g])
                        tensor.wait_ge(qFW2, qfw2_cnt(c, g))
                        tensor.wait_ge(sV, v_xnf_list[c])
                        tensor.wait_ge(sA, v_bcev_last)
                        if f >= 2:
                            tensor.wait_ge(sA, mF_gelu[f - 2])
                        elif x2a_prev is not None:
                            tensor.wait_ge(sA, x2a_prev)
                        for k in range(NB):
                            tensor.matmul(
                                out=pmm[bank][:, 0:TC],
                                lhsT=fcw[:, (wslot * NB + k) * 512 + fi * 128:
                                         (wslot * NB + k) * 512 + (fi + 1) * 128],
                                rhs=xnfc[:, k * TC:(k + 1) * TC],
                                start=(k == 0), stop=False)
                        inc(tensor.matmul(
                            out=pmm[bank][:, 0:TC],
                            lhsT=bfcT[:, f * 128:(f + 1) * 128],
                            rhs=midfc[:, c * TC:(c + 1) * TC],
                            start=False, stop=True), sP, 1)
                fc_grp_ms.append(val(sP))
            v_fc_pe = val(sP)
            v_fcmm_list = list(range(v_pe_f_base + 1, v_pe_f_base + 1 + FB))

            if pipe:
                @block.tensor
                def _(tensor, c=c, mF=mF, v_sa_prev=v_statsF_prev):
                    tensor.wait_ge(sV, v_sa_prev)
                    for k in range(NB):
                        tensor.wait_ge(sA, mF['fsq%d' % k])
                        tensor.matmul(
                            out=pln[0:1, 0:TC], lhsT=ones_bf[:],
                            rhs=x1f[:, k * L + (c + 1) * TC:k * L + (c + 2) * TC],
                            start=(k == 0), stop=(k == NB - 1))
                        inc(tensor.matmul(
                            out=pln[0:1, TC:2 * TC], lhsT=ones_bf[:],
                            rhs=h1s((c + 1) % 2)[:, k * TC:(k + 1) * TC],
                            start=(k == 0), stop=(k == NB - 1)), sP, 1)
                sums_pe[c + 1] = val(sP)

            @block.tensor
            def _(tensor, c=c, v_midp_prev=v_midp_prev, mF_gelu=mF_gelu):
                tensor.wait_ge(sA, mF_gelu[FB - 1])
                if v_midp_prev:
                    tensor.wait_ge(sV, v_midp_prev)
                for fk in range(FB):
                    ins = tensor.matmul(
                        out=pmid[:, 0:TC],
                        lhsT=aprojT[:, fk * N:(fk + 1) * N],
                        rhs=h1s(c % 2)[:, fk * TC:(fk + 1) * TC],
                        start=(fk == 0), stop=(fk == FB - 1))
                inc(ins, sP, 1)
            v_midmm = val(sP)

            @block.vector
            def _(vector, c=c, v_midmm=v_midmm):
                vector.wait_ge(sP, v_midmm)
                inc(vector.tensor_scalar_mul(out=midpj[:], in0=pmid[:, 0:TC],
                                             scalar1=LORA_SCALE), sV, 1)
            v_midp = val(sV)
            v_midp_prev = v_midp

            if pipe:
                v_vf_n = emit_fstats_a(c + 1)

            v_x2_list = []

            def emit_proj(k, c=c, mF_x2a=mF_x2a, v_midp=v_midp):
                wslot = k % 2
                guard = pj_ms[-2] if len(pj_ms) >= 2 else None
                bank = k % 2

                @block.sync
                def _(sync, c=c, k=k, guard=guard, wslot=wslot):
                    if c == 0 and k <= 1:
                        sync.wait_ge(sV, v_attn)
                        sync.wait_ge(qQ[0], v_quant_q0)
                        sync.wait_ge(qQ[1], v_quant_q1)
                    if guard is not None:
                        sync.wait_ge(sP, guard)
                    inc(sync.dma_start(out=pjw_dst_k(wslot, 0, 16),
                                       in_=pjw_src_k(k, 0, 16)), qPW, 16)
                v_pjw = val(qPW)

                @block.tensor
                def _(tensor, c=c, k=k, bank=bank, wslot=wslot,
                      v_pjw=v_pjw, v_midp=v_midp, mF_x2a=mF_x2a):
                    tensor.wait_ge(qPW, v_pjw)
                    tensor.wait_ge(qPW2, qpw2_cnt(c, k))
                    tensor.wait_ge(sV, v_midp)
                    if k >= 2:
                        tensor.wait_ge(sA, mF_x2a[k - 2])
                    for fk in range(FB):
                        tensor.matmul(
                            out=pmm[bank][:, 0:TC],
                            lhsT=pjw[:, wslot * FB * 128 + fk * 128:
                                     wslot * FB * 128 + (fk + 1) * 128],
                            rhs=h1s(c % 2)[:, fk * TC:(fk + 1) * TC],
                            start=(fk == 0), stop=False)
                    inc(tensor.matmul(
                        out=pmm[bank][:, 0:TC],
                        lhsT=bprojT[:, k * 128:(k + 1) * 128],
                        rhs=midpj[:],
                        start=False, stop=True), sP, 1)
                pj_ms.append(val(sP))
                x2a_all.append(mF_x2a[k])

            def emit_x2(k, c=c, mF_x2a=mF_x2a):
                @block.vector
                def _(vector, c=c, k=k, mF_x2a=mF_x2a):
                    vector.wait_ge(sA, mF_x2a[k])
                    xs = x2f[:, (k % 2) * TC:(k % 2 + 1) * TC]
                    inc(vector.tensor_add(
                        out=xs, in0=xs,
                        in1=x1f[:, k * L + c * TC:k * L + (c + 1) * TC]), sV, 1)
                v_x2_list.append(val(sV))
                v_x2 = v_x2_list[-1]

                @block.sync
                def _(sync, c=c, k=k, v_x2=v_x2):
                    sync.wait_ge(sV, v_x2)
                    x2st_all.append((qO[k % 2], inc(sync.dma_start(
                        out=out_h[k * 128:(k + 1) * 128, c * TC:(c + 1) * TC],
                        in_=x2f[:, (k % 2) * TC:(k % 2 + 1) * TC]), qO[k % 2], 16)))

            emit_proj(0)
            emit_proj(1)
            if pipe:
                v_outer_n = emit_fouter(
                    emit_fstats_b(mF['fsqrt']), sV, v_xnf_list[c])
            emit_x2(0)
            emit_x2(1)
            if pipe:
                emit_fapply(c + 1, v_outer_n, v_fc_pe)
                v_statsF_prev = v_vf_n
            for k in range(2, NB):
                emit_proj(k)
                emit_x2(k)

            @block.scalar
            def _(scalar, c=c, mF=mF, order_f=order_f, fcmm=v_fcmm_list,
                  pjm=list(pj_ms[-NB:]), x2st=list(x2st_all),
                  grp_ms=list(fc_grp_ms[-8:]),
                  v_vf_n=(v_vf_n if pipe else 0)):
                for nm in order_f:
                    if nm.startswith('ldf'):
                        g = int(nm[3:])
                        if g <= 7:
                            scalar.wait_ge(sP, grp_ms[g - 2])
                            inc(scalar.dma_start(out=fcw_dst_g(g % 2, 4, 8),
                                                 in_=fcw_src_g(g, 4, 8)), qFW2, 16)
                        else:
                            gg = g - 8
                            scalar.wait_ge(sP, grp_ms[gg + 6])
                            inc(scalar.dma_start(out=fcw_dst_g(gg % 2, 4, 8),
                                                 in_=fcw_src_g(gg, 4, 8)), qFW2, 16)
                    elif nm.startswith('ldp'):
                        k = int(nm[3:])
                        if k <= 7:
                            scalar.wait_ge(sP, pjm[k - 2])
                            inc(scalar.dma_start(out=pjw_dst_k(k % 2, 16, 32),
                                                 in_=pjw_src_k(k, 16, 32)), qPW2, 16)
                        else:
                            kk = k - 8
                            scalar.wait_ge(sP, pjm[kk + 6])
                            inc(scalar.dma_start(out=pjw_dst_k(kk % 2, 16, 32),
                                                 in_=pjw_src_k(kk, 16, 32)), qPW2, 16)
                    elif nm.startswith('gelu'):
                        f = int(nm[4:])
                        scalar.wait_ge(sP, fcmm[f])
                        inc(scalar.activation(
                            out=h1s(c % 2)[:, f * TC:(f + 1) * TC],
                            in_=pmm[(f % 2)][:, 0:TC], func=AF.Gelu,
                            bias=bfc[:, f:f + 1]), sA, 1)
                    elif nm == 'fsqrt':
                        scalar.wait_ge(sV, v_vf_n)
                        inc(scalar.activation(out=SR(1), in_=SR(1), func=AF.Sqrt),
                            sA, 1)
                    elif nm.startswith('fsq'):
                        j = int(nm[3:])
                        inc(scalar.activation(
                            out=h1s((c + 1) % 2)[:, j * TC:(j + 1) * TC],
                            in_=x1f[:, j * L + (c + 1) * TC:j * L + (c + 2) * TC],
                            func=AF.Square), sA, 1)
                    elif nm.startswith('x2a'):
                        k = int(nm[3:])
                        scalar.wait_ge(sP, pjm[k])
                        idx = len(x2st) - NB + k - 2
                        if idx >= 0:
                            smv = x2st[idx]
                            scalar.wait_ge(smv[0], smv[1])
                        inc(scalar.activation(
                            out=x2f[:, (k % 2) * TC:(k % 2 + 1) * TC],
                            in_=pmm[(k % 2)][:, 0:TC],
                            func=AF.Identity, bias=bproj[:, k:k + 1]), sA, 1)
                    else:
                        raise AssertionError(nm)
            assert val(sA) == F_end, (c, val(sA), F_end)

        v_out0 = val(qO[0])
        v_out1 = val(qO[1])

        @block.vector
        def _(vector):
            vector.wait_ge(qO[0], v_out0)
            vector.wait_ge(qO[1], v_out1)

    return nc


def _mk_repc():
    import ml_dtypes
    r = np.zeros((32, 32 * 128), dtype=ml_dtypes.bfloat16)
    for j in range(32):
        row = (j // 2) if j % 2 == 0 else 16 + (j // 2)
        r[row, j * 128:(j + 1) * 128] = 1
    return r


def kernel(**inputs):
    ins = {k: np.asarray(v) for k, v in inputs.items()}
    x = ins["L0_attn_x"]
    n_cores = 8

    common = dict(
        wd=np.ascontiguousarray(ins["L0_attn_W_delta"]),
        wbc=np.ascontiguousarray(
            np.concatenate([ins["L0_attn_W_B"], ins["L0_attn_W_C"]], axis=1)),
        afc=np.ascontiguousarray(ins["L0_ffn_A_fc"].T),
        bfcT=np.ascontiguousarray(ins["L0_ffn_B_fc"].T),
        wfcT=np.ascontiguousarray(ins["L0_ffn_W_fc"].T),
        wprojT=np.ascontiguousarray(ins["L0_ffn_W_proj"].T),
        aprojT=np.ascontiguousarray(ins["L0_ffn_A_proj"].T),
        bprojT=np.ascontiguousarray(ins["L0_ffn_B_proj"].T),
        alog=np.ascontiguousarray(ins["L0_attn_A_log"]),
        gam=np.ascontiguousarray(ins["L0_attn_gamma"][:, None]),
        bet=np.ascontiguousarray(ins["L0_attn_beta"][:, None]),
        dskip=np.ascontiguousarray(ins["L0_attn_D_skip"][:, None]),
        bd=np.ascontiguousarray(ins["L0_attn_b_delta"][:, None]),
        bbc=np.ascontiguousarray(
            np.concatenate([ins["L0_attn_b_B"], ins["L0_attn_b_C"]])[:, None]),
        bfc=np.ascontiguousarray(ins["L0_ffn_b_fc"][:, None]),
        bproj=np.ascontiguousarray(ins["L0_ffn_b_proj"][:, None]),
        ident=np.eye(128, dtype=np.float32),
        sel2=np.concatenate([
            np.concatenate([np.ones((1, 128), np.float32), np.zeros((1, 128), np.float32)]),
            np.concatenate([np.zeros((1, 128), np.float32), np.ones((1, 128), np.float32)]),
        ], axis=1),
        repc=_mk_repc(),
    )

    nc = build_program()
    in_maps = []
    for b in range(n_cores):
        m = dict(common)
        m["x0"] = np.ascontiguousarray(x[b].T)
        in_maps.append(m)
    res = run_bass_kernel_spmd(nc, in_maps, core_ids=list(range(n_cores)))
    out = np.stack([np.asarray(r["out"]).T for r in res.results], axis=0)
    return np.ascontiguousarray(out.astype(np.float32))


# version 25
# speedup vs baseline: 1.4205x; 1.0284x over previous
import contextlib
import numpy as np

import concourse.bass as bass
import concourse.mybir as mybir
from concourse.bass_utils import run_bass_kernel_spmd

F32 = mybir.dt.float32
BF16 = mybir.dt.bfloat16
I32 = mybir.dt.int32
AF = mybir.ActivationFunctionType
ALU = mybir.AluOpType
AX = mybir.AxisListType

D = 1024
L = 2048
N = 16
FF = 4096
NB = D // 128
FB = FF // 128
TC = 512
NCH = L // TC
EPS = 1e-5
LORA_SCALE = 2.0
QMAX = 127.0
NDA = 8
NDBX = 4


def rep_row(src_ap, width):
    pitch = src_ap.ap[0][0]
    return bass.AP(src_ap.tensor, src_ap.offset, [[pitch, 1], [0, 128], [1, width]])


def build_program():
    nc = bass.Bass("TRN2", target_bir_lowering=False,
                   detect_race_conditions=False)

    def din(name, shape):
        return nc.dram_tensor(name, shape, F32, kind="ExternalInput").ap()

    x0_h = din("x0", [D, L])
    wd_h = din("wd", [D, D])
    wbc_h = din("wbc", [D, 32])
    afc_h = din("afc", [D, N])
    bfcT_h = din("bfcT", [N, FF])
    wfcT_h = din("wfcT", [D, FF])
    wprojT_h = din("wprojT", [FF, D])
    aprojT_h = din("aprojT", [FF, N])
    bprojT_h = din("bprojT", [N, D])
    alog_h = din("alog", [D, N])
    gam_h = din("gam", [D, 1])
    bet_h = din("bet", [D, 1])
    dskip_h = din("dskip", [D, 1])
    bd_h = din("bd", [D, 1])
    bbc_h = din("bbc", [32, 1])
    bfc_h = din("bfc", [FF, 1])
    bproj_h = din("bproj", [D, 1])
    ident_h = din("ident", [128, 128])
    sel2_h = din("sel2", [2, 256])
    repc_h = nc.dram_tensor("repc", [32, 32 * 128], BF16, kind="ExternalInput").ap()

    out_h = nc.dram_tensor("out", [D, L], F32, kind="ExternalOutput").ap()
    qfc_h = nc.dram_tensor("qfc_scr", [D, FF], BF16, kind="Internal").ap()
    qproj_h = nc.dram_tensor("qproj_scr", [FF, D], BF16, kind="Internal").ap()

    cnt = {}

    def inc(instr, sem, n=1):
        instr.then_inc(sem, n)
        cnt[sem.name] = cnt.get(sem.name, 0) + n
        return cnt[sem.name]

    def val(sem):
        return cnt.get(sem.name, 0)

    with contextlib.ExitStack() as ctx:
        E = ctx.enter_context
        sb = lambda name, shape, dt=F32: E(nc.sbuf_tensor(name, shape, dt))

        wdT = sb("s_wdT", [128, NB * D], BF16)
        wbc = sb("s_wbc", [128, NB * 32], BF16)
        afc = sb("s_afc", [128, NB * N], BF16)
        bfcT = sb("s_bfcT", [N, FF], BF16)
        aprojT = sb("s_aprojT", [128, FB * N], BF16)
        bprojT = sb("s_bprojT", [N, D], BF16)
        aneg = sb("s_aneg", [128, NB * N])
        gam = sb("s_gam", [128, NB])
        bet = sb("s_bet", [128, NB])
        dskip = sb("s_dskip", [128, NB])
        bd = sb("s_bd", [128, NB])
        bbc = sb("s_bbc", [32, 1])
        bfc = sb("s_bfc", [128, FB])
        bproj = sb("s_bproj", [128, NB])
        ones_bf = sb("s_ones", [128, 1], BF16)
        ident_bf = sb("s_ident", [128, 128], BF16)
        sel2 = sb("s_sel2", [2, 256])
        statAB = sb("s_statAB", [2, TC])
        statPQ = sb("s_statPQ", [2, TC])
        repc = sb("s_repc", [32, 32 * 128], BF16)
        onesf = sb("s_onesf", [128, 1])
        z1 = sb("s_z1", [128, 1])
        epsb = sb("s_epsb", [1, 1])
        amax_col = sb("s_amaxc", [128, 2])
        amax_rows = sb("s_amaxrows", [128, 40])
        qsc = sb("s_qsc", [1, 4])
        qsc_b = sb("s_qscb", [128, 4])
        stat = sb("s_stat", [1, 3 * TC])
        amax_row = stat[0:1, 0:256]
        bca = sb("s_bca", [128, 2 * TC])
        midpj = None
        bcm = sb("s_bcm", [32, L], BF16)
        midfc = sb("s_midfc", [N, L], BF16)
        carry = sb("s_carry", [128, NB * N])
        x2c = sb("s_x2c", [128, 2 * TC], BF16)
        x1f = sb("s_x1f", [128, NB * L], BF16)

        A1 = sb("s_arena1", [128, 20480])
        A2 = sb("s_arena2", [128, 2 * NB * TC])

        a1 = A1[:]
        alog = a1[:, 12288:12416]
        wstg = a1[:, 0:8192]
        amax_acc = a1[:, 8192:12288]
        qi32 = a1[:, 12288:16384].bitcast(I32)
        qbf = a1[:, 16384:20480].bitcast(BF16)
        brep = a1[:, 0:4096].bitcast(BF16)
        crep = a1[:, 4096:8192].bitcast(BF16)
        hbuf = a1[:, 8192:12288].bitcast(BF16)
        xn2c = a1[:, 12288:14336].bitcast(BF16)
        dltF = a1[:, 14336:16384].bitcast(BF16)
        uu = a1[:, 16384:16640].bitcast(BF16)
        tln = a1[:, 16640:17152]
        dbx2 = a1[:, 17152:18176].bitcast(BF16)
        da = a1[:, 18176:20224].bitcast(BF16)

        h1c = a1[:, 0:16384].bitcast(BF16)
        fcw = a1[:, 16384:20480].bitcast(BF16)

        a2 = A2[:]
        pjw = a2[:, 0:4096].bitcast(BF16)
        xnfc = a2[:, 4096:6144].bitcast(BF16)
        x2f = a2[:, 6144:7168]
        aptmp = a2[:, 7168:7680]
        stage = a2

        def x0t(slot, k):
            return a2[:, (slot * NB + k) * TC:(slot * NB + k + 1) * TC]

        pmm = [E(nc.psum_tensor("p_mm0", [128, 512], F32)),
               E(nc.psum_tensor("p_mm1", [128, 512], F32))]
        pmid = E(nc.psum_tensor("p_mid", [N, 512], F32))
        pln = E(nc.psum_tensor("p_ln", [1, 2 * TC], F32))
        pyc = E(nc.psum_tensor("p_yc", [128, 512], F32))
        prepA = E(nc.psum_tensor("p_repA", [128, 512], F32))
        prepB = E(nc.psum_tensor("p_repB", [128, 512], F32))

        qC = E(nc.semaphore("qC"))
        qW = [E(nc.semaphore("qW0")), E(nc.semaphore("qW1"))]
        qQ = [E(nc.semaphore("qQ0")), E(nc.semaphore("qQ1"))]
        qR = E(nc.semaphore("qR"))
        qX = [E(nc.semaphore("qX0")), E(nc.semaphore("qX1"))]
        qB = E(nc.semaphore("qB"))
        qFW = E(nc.semaphore("qFW"))
        qFW2 = E(nc.semaphore("qFW2"))
        qPW = E(nc.semaphore("qPW"))
        qPW2 = E(nc.semaphore("qPW2"))
        qO = [E(nc.semaphore("qO0")), E(nc.semaphore("qO1"))]
        sV = E(nc.semaphore("sV"))
        sY = E(nc.semaphore("sY"))
        sA = E(nc.semaphore("sA"))
        sP = E(nc.semaphore("sP"))
        sG = E(nc.semaphore("sG"))


        block = E(nc.Block())

        def SR(i):
            return stat[0:1, i * TC:(i + 1) * TC]

        @block.sync
        def _(sync):
            ctxnc = nc.allow_non_contiguous_dma(reason="tiny per-dblock vectors")
            ctxnc.__enter__()
            inc(sync.dma_start(out=stage[:, 0:NB * D].rearrange("p (b d) -> p b d", b=NB), in_=wd_h.rearrange("(b p) d -> p b d", p=128)), qC, 16)
            inc(sync.dma_start(out=alog[:].rearrange("p (b n) -> p b n", b=NB), in_=alog_h.rearrange("(b p) n -> p b n", p=128)), qC, 16)
            inc(sync.dma_start(out=gam[:].unsqueeze(2), in_=gam_h.rearrange("(b p) o -> p b o", p=128)), qC, 16)
            inc(sync.dma_start(out=bet[:].unsqueeze(2), in_=bet_h.rearrange("(b p) o -> p b o", p=128)), qC, 16)
            inc(sync.dma_start(out=dskip[:].unsqueeze(2), in_=dskip_h.rearrange("(b p) o -> p b o", p=128)), qC, 16)
            inc(sync.dma_start(out=bd[:].unsqueeze(2), in_=bd_h.rearrange("(b p) o -> p b o", p=128)), qC, 16)
            inc(sync.dma_start(out=bbc[:], in_=bbc_h), qC, 16)
            inc(sync.dma_start(out=bfc[:].unsqueeze(2), in_=bfc_h.rearrange("(b p) o -> p b o", p=128)), qC, 16)
            inc(sync.dma_start(out=bproj[:].unsqueeze(2), in_=bproj_h.rearrange("(b p) o -> p b o", p=128)), qC, 16)
            ctxnc.__exit__(None, None, None)
        v_c0 = val(qC)

        @block.vector
        def _(vector):
            vector.wait_ge(qC, v_c0)
            vector.memset(ones_bf[:], 1.0)
            vector.memset(onesf[:], 1.0)
            vector.memset(z1[:], 0.0)
            vector.memset(epsb[:], EPS)
            vector.memset(carry[:], 0.0)
            inc(vector.tensor_copy(out=wdT[:], in_=stage[:, 0:NB * D]), sV, 1)
        v_wdT = val(sV)

        @block.scalar
        def _(scalar):
            scalar.wait_ge(sV, v_wdT)
            inc(scalar.activation(out=stage[:, 0:NB * N], in_=alog[:], func=AF.Exp), sA, 1)
        v_aexp = val(sA)

        @block.vector
        def _(vector):
            vector.wait_ge(sA, v_aexp)
            inc(vector.tensor_scalar_mul(out=aneg[:], in0=stage[:, 0:NB * N], scalar1=-1.0), sV, 1)
        v_aneg = val(sV)

        @block.sync
        def _(sync):
            sync.wait_ge(sV, v_aneg)
            inc(sync.dma_start(out=stage[:, 0:NB * 32].rearrange("p (b d) -> p b d", b=NB), in_=wbc_h.rearrange("(b p) d -> p b d", p=128)), qC, 16)
            inc(sync.dma_start(out=stage[:, NB * 32:NB * 48].rearrange("p (b n) -> p b n", b=NB), in_=afc_h.rearrange("(b p) n -> p b n", p=128)), qC, 16)
            inc(sync.dma_start(out=stage[:, 512:512 + FB * N].rearrange("p (b n) -> p b n", b=FB), in_=aprojT_h.rearrange("(b p) n -> p b n", p=128)), qC, 16)
            inc(sync.dma_start(out=stage[0:N, 1024:1024 + FF], in_=bfcT_h), qC, 16)
            inc(sync.dma_start(out=stage[0:N, 5120:5120 + D], in_=bprojT_h), qC, 16)
            inc(sync.dma_start(out=stage[:, 6144:6144 + 128], in_=ident_h), qC, 16)
            inc(sync.dma_start(out=sel2[:], in_=sel2_h), qC, 16)
            inc(sync.dma_start(out=repc[:], in_=repc_h), qC, 16)
        v_c1 = val(qC)

        @block.vector
        def _(vector):
            vector.wait_ge(qC, v_c1)
            vector.tensor_copy(out=wbc[:], in_=stage[:, 0:NB * 32])
            vector.tensor_copy(out=afc[:], in_=stage[:, NB * 32:NB * 48])
            vector.tensor_copy(out=aprojT[:], in_=stage[:, 512:512 + FB * N])
            vector.tensor_scalar_mul(out=bfcT[:], in0=stage[0:N, 1024:1024 + FF], scalar1=LORA_SCALE)
            vector.tensor_scalar_mul(out=bprojT[:], in0=stage[0:N, 5120:5120 + D], scalar1=LORA_SCALE)
            inc(vector.tensor_copy(out=ident_bf[:], in_=stage[:, 6144:6144 + 128]), sV, 1)
        v_consts = val(sV)

        row_ms = []
        for wi, (w_h, klen, rows) in enumerate(((wfcT_h, FF, NB), (wprojT_h, D, FB))):
            for r in range(rows):
                g = len(row_ms)
                slot = g % 2

                if slot == 0:
                    @block.sync
                    def _(sync, g=g, slot=slot, w_h=w_h, klen=klen, r=r):
                        if g >= 2:
                            sync.wait_ge(sV, row_ms[g - 2])
                        if g == 0:
                            sync.wait_ge(sV, v_consts)
                        inc(sync.dma_start(out=wstg[:, slot * FF:slot * FF + klen],
                                           in_=w_h[r * 128:(r + 1) * 128, :]), qW[slot], 16)
                else:
                    @block.scalar
                    def _(scalar, g=g, slot=slot, w_h=w_h, klen=klen, r=r):
                        if g >= 2:
                            scalar.wait_ge(sV, row_ms[g - 2])
                        if g == 1:
                            scalar.wait_ge(sV, v_consts)
                        inc(scalar.dma_start(out=wstg[:, slot * FF:slot * FF + klen],
                                             in_=w_h[r * 128:(r + 1) * 128, :]), qW[slot], 16)
                v_ld = val(qW[slot])

                @block.vector
                def _(vector, g=g, slot=slot, klen=klen, v_ld=v_ld):
                    vector.wait_ge(qW[slot], v_ld)
                    row_ms.append(inc(vector.tensor_reduce(
                        out=amax_rows[:, g:g + 1],
                        in_=wstg[:, slot * FF:slot * FF + klen],
                        axis=AX.X, op=ALU.max, apply_absolute_value=True), sV, 1))

        @block.vector
        def _(vector):
            vector.wait_ge(sV, row_ms[-1])
            vector.tensor_reduce(out=amax_col[:, 0:1], in_=amax_rows[:, 0:NB],
                                 axis=AX.X, op=ALU.max)
            inc(vector.tensor_reduce(out=amax_col[:, 1:2], in_=amax_rows[:, NB:40],
                                     axis=AX.X, op=ALU.max), sV, 1)
        v_amax = val(sV)

        @block.sync
        def _(sync):
            sync.wait_ge(sV, v_amax)
            with nc.allow_non_contiguous_dma(reason="two 128-element transposes, once"):
                inc(sync.dma_start(out=amax_row[0:1, 0:128], in_=amax_col[:, 0:1]), qC, 16)
                inc(sync.dma_start(out=amax_row[0:1, 128:256], in_=amax_col[:, 1:2]), qC, 16)
        v_arow = val(qC)

        @block.vector
        def _(vector):
            vector.wait_ge(qC, v_arow)
            vector.tensor_reduce(out=qsc[0:1, 0:1], in_=amax_row[0:1, 0:128], axis=AX.X, op=ALU.max)
            vector.tensor_reduce(out=qsc[0:1, 2:3], in_=amax_row[0:1, 128:256], axis=AX.X, op=ALU.max)
            inc(vector.engine_nop(), sV, 1)
        v_maxv = val(sV)

        @block.vector
        def _(vector):
            vector.wait_ge(sV, v_maxv)
            vector.tensor_scalar_mul(out=qsc[0:1, 0:1], in0=qsc[0:1, 0:1], scalar1=1.0 / QMAX)
            inc(vector.tensor_scalar_mul(out=qsc[0:1, 2:3], in0=qsc[0:1, 2:3], scalar1=1.0 / QMAX), sV, 1)
        v_scale = val(sV)

        @block.vector
        def _(vector):
            vector.wait_ge(sV, v_scale)
            inc(vector.reciprocal(out=qsc[0:1, 1:2], in_=qsc[0:1, 0:1]), sV, 1)
            inc(vector.reciprocal(out=qsc[0:1, 3:4], in_=qsc[0:1, 2:3]), sV, 1)
        v_inv = val(sV)

        @block.sync
        def _(sync):
            sync.wait_ge(sV, v_inv)
            inc(sync.dma_start(out=qsc_b[:].unsqueeze(1), in_=rep_row(qsc[0:1, 0:4], 4)), qC, 16)
        v_qscb = val(qC)

        q_dve = []
        q_st = []
        for wi, (w_h, q_h, klen, rows) in enumerate(
                ((wfcT_h, qfc_h, FF, NB), (wprojT_h, qproj_h, D, FB))):
            inv_ap = qsc_b[:, 2 * wi + 1:2 * wi + 2]
            sc_ap = qsc_b[:, 2 * wi:2 * wi + 1]
            for r in range(rows):
                g = len(q_dve)
                slot = g % 2

                if slot == 0:
                    @block.sync
                    def _(sync, g=g, slot=slot, w_h=w_h, klen=klen, r=r):
                        if g >= 2:
                            sync.wait_ge(sV, q_dve[g - 2])
                        else:
                            sync.wait_ge(sV, row_ms[-1])
                        inc(sync.dma_start(out=wstg[:, slot * FF:slot * FF + klen],
                                           in_=w_h[r * 128:(r + 1) * 128, :]), qW[slot], 16)
                else:
                    @block.scalar
                    def _(scalar, g=g, slot=slot, w_h=w_h, klen=klen, r=r):
                        if g >= 2:
                            scalar.wait_ge(sV, q_dve[g - 2])
                        else:
                            scalar.wait_ge(sV, row_ms[-1])
                        inc(scalar.dma_start(out=wstg[:, slot * FF:slot * FF + klen],
                                             in_=w_h[r * 128:(r + 1) * 128, :]), qW[slot], 16)
                v_ld = val(qW[slot])

                @block.vector
                def _(vector, g=g, slot=slot, klen=klen, v_ld=v_ld,
                      inv_ap=inv_ap, sc_ap=sc_ap):
                    vector.wait_ge(qW[slot], v_ld)
                    vector.wait_ge(qC, v_qscb)
                    if g >= 2:
                        vector.wait_ge(qQ[slot], q_st[g - 2])
                    w = wstg[:, slot * FF:slot * FF + klen]
                    vector.tensor_scalar(out=qi32[:, 0:klen], in0=w, scalar1=inv_ap,
                                         scalar2=None, op0=ALU.mult)
                    q_dve.append(inc(vector.tensor_scalar(
                        out=qbf[:, slot * FF:slot * FF + klen], in0=qi32[:, 0:klen],
                        scalar1=sc_ap, scalar2=None, op0=ALU.mult), sV, 1))

                if slot == 0:
                    @block.sync
                    def _(sync, g=g, slot=slot, q_h=q_h, klen=klen, r=r):
                        sync.wait_ge(sV, q_dve[g])
                        q_st.append(inc(sync.dma_start(
                            out=q_h[r * 128:(r + 1) * 128, :],
                            in_=qbf[:, slot * FF:slot * FF + klen]), qQ[slot], 16))
                else:
                    @block.scalar
                    def _(scalar, g=g, slot=slot, q_h=q_h, klen=klen, r=r):
                        scalar.wait_ge(sV, q_dve[g])
                        q_st.append(inc(scalar.dma_start(
                            out=q_h[r * 128:(r + 1) * 128, :],
                            in_=qbf[:, slot * FF:slot * FF + klen]), qQ[slot], 16))
        v_quant_q0 = val(qQ[0])
        v_quant_q1 = val(qQ[1])
        v_quant_v = val(sV)

        midpj = bca[:].bitcast(BF16)[0:N, 0:TC]
        P1B = bca[:, 0:TC]
        Q1B = bca[:, TC:2 * TC]
        CB = bca[:, 0:TC]
        ALB = bca[:, TC:2 * TC]

        v_x1_done = []
        v_x1_list = []
        v_yred_list = []
        v_yredseg_list = []
        v_exp_list = []
        v_u_list = []
        v_ldx_by_chunk = []
        sums_base = {}
        v_m1_by_chunk = {}
        v_scan_last = 0
        mA_sq_saved = None
        prev_prep_guards = None
        prev_bcm_mid = 0

        def emit_x0_load(c):
            slot = c % 2

            @block.sync
            def _(sync, c=c, slot=slot):
                if c >= 2:
                    sync.wait_ge(sV, v_x1_done[c - 2])
                dst = a2[:, slot * NB * TC:(slot + 1) * NB * TC].rearrange(
                    "p (k t) -> p k t", k=NB)
                srcv = x0_h.rearrange("(k p) t -> p k t", p=128)[:, :, c * TC:(c + 1) * TC]
                inc(sync.dma_start(out=dst, in_=srcv), qX[slot], 16)
            v_ldx_by_chunk.append(val(qX[slot]))

        emit_x0_load(0)
        emit_x0_load(1)

        for c in range(NCH):
            slot = c % 2
            v_scan_last_entry = v_scan_last

            if c == 0:
                P0 = val(sA)
                mA_sq = [P0 + 1 + k for k in range(NB)]
                pe_base0 = val(sP)

                @block.scalar
                def _(scalar, pe_base0=pe_base0):
                    scalar.wait_ge(qX[0], v_ldx_by_chunk[0])
                    scalar.wait_ge(sV, v_quant_v)
                    scalar.wait_ge(qQ[0], v_quant_q0)
                    scalar.wait_ge(qQ[1], v_quant_q1)
                    for k in range(NB):
                        if k >= 2:
                            scalar.wait_ge(sP, pe_base0 + k - 1)
                        inc(scalar.activation(
                            out=x2c[:, (k % 2) * TC:(k % 2 + 1) * TC],
                            in_=x0t(0, k), func=AF.Square), sA, 1)

                sums_base[0] = val(sP)

                @block.tensor
                def _(tensor, mA_sq=mA_sq):
                    tensor.wait_ge(sV, v_consts)
                    for k in range(NB):
                        tensor.wait_ge(sA, mA_sq[k])
                        tensor.matmul(out=pln[0:1, 0:TC], lhsT=onesf[:],
                                      rhs=x0t(0, k),
                                      start=(k == 0), stop=(k == NB - 1))
                        inc(tensor.matmul(out=pln[0:1, TC:2 * TC], lhsT=ones_bf[:],
                                          rhs=x2c[:, (k % 2) * TC:(k % 2 + 1) * TC],
                                          start=(k == 0), stop=(k == NB - 1)), sP, 1)
            else:
                mA_sq = mA_sq_saved

            A0 = val(sA)
            nsq = NB if c + 1 < NCH else 0
            order = (['sqrt', 'exp0', 'ln0', 'bcm', 'mid']
                     + ['ev%d' % j for j in range(4)]
                     + ['da_0_%d' % n for n in range(8)]
                     + ['ev%d' % j for j in range(4, 16)]
                     + ['p1ev', 'q1ev']
                     + ['ev%d' % j for j in range(16, 32)]
                     + ['da_0_%d' % n for n in range(8, 12)] + ['cp_0_0']
                     + ['da_0_%d' % n for n in range(12, 16)] + ['cp_0_1'])
            order += ['sq%d' % j for j in range(nsq)]
            for k in range(1, NB):
                order += ['exp%d' % k, 'ln%d' % k]
                for n in range(N):
                    order += ['da_%d_%d' % (k, n)]
                    if n == 2:
                        order += ['cp_%d_2' % (k - 1)]
                    elif n == 6:
                        order += ['cp_%d_3' % (k - 1)]
                    elif n == 11:
                        order += ['cp_%d_0' % k]
                    elif n == 15:
                        order += ['cp_%d_1' % k]
            order += ['cp_7_2', 'cp_7_3']
            mA = {nm: A0 + 1 + i for i, nm in enumerate(order)}
            A_end = A0 + len(order)
            mA_sqrt = mA['sqrt']
            mA_exp = [mA['exp%d' % k] for k in range(NB)]
            mA_ln = [mA['ln%d' % k] for k in range(NB)]
            mA_da = [[mA['da_%d_%d' % (k, n)] for n in range(N)] for k in range(NB)]
            m_evac = [mA['ev%d' % j] for j in range(32)]
            m_p1ev, m_q1ev = mA['p1ev'], mA['q1ev']
            mA_sq_next = [mA['sq%d' % j] for j in range(nsq)]
            mA_sq_saved = mA_sq_next
            mA_cp = [[mA['cp_%d_%d' % (k, q)] for q in range(4)] for k in range(NB)]

            v_sums = sums_base[c] + NB

            @block.vector
            def _(vector, v_sums=v_sums):
                vector.wait_ge(sP, v_sums)
                vector.tensor_scalar_mul(out=SR(0), in0=pln[0:1, 0:TC], scalar1=1.0 / D)
                inc(vector.tensor_scalar_mul(out=SR(1), in0=pln[0:1, TC:2 * TC],
                                             scalar1=1.0 / D), sV, 1)
            v_m1 = val(sV)
            v_m1_by_chunk[c] = v_m1

            @block.vector
            def _(vector, v_m1=v_m1):
                vector.tensor_mul(out=SR(2), in0=SR(0), in1=SR(0))
                vector.tensor_sub(out=SR(1), in0=SR(1), in1=SR(2))
                vector.tensor_scalar(out=SR(2), in0=SR(1), scalar1=1.0 + EPS,
                                     scalar2=EPS * EPS, op0=ALU.mult,
                                     op1=ALU.add)
                inc(vector.tensor_scalar_add(out=SR(1), in0=SR(1),
                                             scalar1=EPS), sV, 1)
            v_v1 = val(sV)

            @block.vector
            def _(vector, mA_sqrt=mA_sqrt):
                vector.wait_ge(sA, mA_sqrt)
                inc(vector.reciprocal(out=stat[0:1, 1 * TC:3 * TC],
                                      in_=stat[0:1, 1 * TC:3 * TC]), sV, 1)
            v_rv = val(sV)

            @block.sync
            def _(sync, v_rv=v_rv):
                sync.wait_ge(sV, v_rv)
                inc(sync.dma_start(out=statAB[0:1, :], in_=SR(0)), qB, 16)
                inc(sync.dma_start(out=statAB[1:2, :], in_=SR(2)), qB, 16)
            v_ab = val(qB)

            @block.vector
            def _(vector, v_ab=v_ab):
                vector.wait_ge(qB, v_ab)
                vector.tensor_mul(out=SR(2), in0=SR(0), in1=SR(1))
                inc(vector.tensor_scalar_add(out=SR(1), in0=SR(1),
                                             scalar1=1.0), sV, 1)
            v_stats = val(sV)

            @block.sync
            def _(sync, v_stats=v_stats):
                sync.wait_ge(sV, v_stats)
                inc(sync.dma_start(out=statPQ[0:1, :], in_=SR(1)), qB, 16)
                inc(sync.dma_start(out=statPQ[1:2, :], in_=SR(2)), qB, 16)
            v_pq = val(qB)

            @block.tensor
            def _(tensor, c=c, v_ab=v_ab, pg=prev_prep_guards):
                tensor.wait_ge(qB, v_ab)
                if pg is not None:
                    tensor.wait_ge(sA, pg[1])
                tensor.matmul(out=prepA[:, 0:TC], lhsT=sel2[:, 0:128], rhs=statAB[:],
                              start=True, stop=True)
                inc(tensor.matmul(out=prepB[:, 0:TC], lhsT=sel2[:, 128:256], rhs=statAB[:],
                                  start=True, stop=True), sP, 1)
            v_cbmm = val(sP)

            v_xn2pre = []

            @block.vector
            def _(vector, c=c, slot=slot, v_cbmm=v_cbmm):
                vector.wait_ge(sP, v_cbmm)
                if c == 0:
                    vector.wait_ge(qQ[0], v_quant_q0)
                    vector.wait_ge(qQ[1], v_quant_q1)
                else:
                    vector.wait_ge(sV, v_x1_done[c - 1])
                for k in range(NB):
                    vector.tensor_sub(out=tln, in0=x0t(slot, k), in1=prepA[:, 0:TC])
                    vector.scalar_tensor_tensor(
                        out=tln, in0=tln, scalar=gam[:, k:k + 1], in1=prepB[:, 0:TC],
                        op0=ALU.mult, op1=ALU.mult)
                    v_xn2pre.append(inc(vector.tensor_scalar(
                        out=xn2c[:, k * TC:(k + 1) * TC], in0=tln,
                        scalar1=bet[:, k:k + 1], scalar2=None,
                        op0=ALU.add), sV, 1))

            gk0 = c * NB
            v_exp_list.append(mA_exp[0])

            @block.tensor
            def _(tensor, c=c, xn2pre=v_xn2pre, pbm=prev_bcm_mid, gk0=gk0):
                if pbm:
                    tensor.wait_ge(sA, pbm)
                if gk0 >= 2:
                    tensor.wait_ge(sA, v_exp_list[gk0 - 2])
                if gk0 >= 1:
                    tensor.wait_ge(sA, v_exp_list[gk0 - 1])
                for k in range(NB):
                    tensor.wait_ge(sV, xn2pre[k])
                    tensor.matmul(out=pmm[1][0:32, 0:TC], lhsT=wbc[:, k * 32:(k + 1) * 32],
                                  rhs=xn2c[:, k * TC:(k + 1) * TC],
                                  start=(k == 0), stop=(k == NB - 1))
                    tensor.matmul(out=pmid[:, 0:TC], lhsT=afc[:, k * N:(k + 1) * N],
                                  rhs=xn2c[:, k * TC:(k + 1) * TC],
                                  start=(k == 0), stop=(k == NB - 1))
                    ins = tensor.matmul(
                        out=pmm[0][:, 0:TC],
                        lhsT=wdT[:, k * D + 0 * 128:k * D + 1 * 128],
                        rhs=xn2c[:, k * TC:(k + 1) * TC],
                        start=(k == 0), stop=(k == NB - 1))
                    if k == NB - 1:
                        inc(ins, sP, 1)
            v_bcmm = val(sP)

            rep_sp = {}

            @block.tensor
            def _(tensor, c=c, mA=mA, m_evac=m_evac, xn2pre=v_xn2pre,
                  v_scans_prev=(v_u_list[-1] + N if c >= 1 else 0),
                  v_ym_prev=v_scan_last_entry):
                tensor.wait_ge(sA, mA['bcm'])
                if c == 0:
                    tensor.wait_ge(sV, v_quant_v)
                else:
                    tensor.wait_ge(sV, v_scans_prev)
                    tensor.wait_ge(sY, v_ym_prev)
                for j in range(16):
                    if j < 2:
                        tensor.wait_ge(sV, xn2pre[NB - 1])
                    else:
                        tensor.wait_ge(sA, m_evac[j - 2])
                    rep_sp[j] = inc(tensor.matmul(
                        out=(prepA if j % 2 == 0 else prepB)[:, 0:TC],
                        lhsT=repc[:, j * 128:(j + 1) * 128],
                        rhs=bcm[:, c * TC:(c + 1) * TC],
                        start=True, stop=True), sP, 1)

            @block.tensor
            def _(tensor, m_evac=m_evac, v_pq=v_pq):
                tensor.wait_ge(qB, v_pq)
                tensor.wait_ge(sA, m_evac[14])
                sp1 = inc(tensor.matmul(out=prepA[:, 0:TC], lhsT=sel2[:, 0:128],
                                        rhs=statPQ[:], start=True, stop=True), sP, 1)
                tensor.wait_ge(sA, m_evac[15])
                sq1 = inc(tensor.matmul(out=prepB[:, 0:TC], lhsT=sel2[:, 128:256],
                                        rhs=statPQ[:], start=True, stop=True), sP, 1)
                rep_sp['p1'] = sp1
                rep_sp['q1'] = sq1

            @block.tensor
            def _(tensor, c=c, mA=mA, m_evac=m_evac):
                for j in range(16, 32):
                    if j == 16:
                        tensor.wait_ge(sA, mA['p1ev'])
                    elif j == 17:
                        tensor.wait_ge(sA, mA['q1ev'])
                    else:
                        tensor.wait_ge(sA, m_evac[j - 2])
                    rep_sp[j] = inc(tensor.matmul(
                        out=(prepA if j % 2 == 0 else prepB)[:, 0:TC],
                        lhsT=repc[:, j * 128:(j + 1) * 128],
                        rhs=bcm[:, c * TC:(c + 1) * TC],
                        start=True, stop=True), sP, 1)
            prev_prep_guards = (m_evac[30], m_evac[31])
            prev_bcm_mid = mA['mid']

            if c + 1 < NCH:
                sums_base[c + 1] = val(sP)

                @block.tensor
                def _(tensor, c=c, mA_sq_next=mA_sq_next, v_m1=v_m1,
                      nslot=(c + 1) % 2):
                    tensor.wait_ge(sV, v_m1)
                    for k in range(NB):
                        tensor.wait_ge(sA, mA_sq_next[k])
                        tensor.matmul(out=pln[0:1, 0:TC], lhsT=onesf[:],
                                      rhs=x0t(nslot, k),
                                      start=(k == 0), stop=(k == NB - 1))
                        inc(tensor.matmul(out=pln[0:1, TC:2 * TC], lhsT=ones_bf[:],
                                          rhs=x2c[:, (k % 2) * TC:(k % 2 + 1) * TC],
                                          start=(k == 0), stop=(k == NB - 1)), sP, 1)

            v_dmm_list = [v_bcmm]

            def emit_delta(k, c=c):
                gk = c * NB + k
                mmslot = k % 2

                @block.tensor
                def _(tensor, c=c, k=k, gk=gk, mmslot=mmslot, mA=mA):
                    if k == 1:
                        tensor.wait_ge(sA, mA['bcm'])
                    if gk >= 2:
                        tensor.wait_ge(sA, v_exp_list[gk - 2])
                    for kk in range(NB):
                        ins = tensor.matmul(
                            out=pmm[mmslot][:, 0:TC],
                            lhsT=wdT[:, kk * D + k * 128:kk * D + (k + 1) * 128],
                            rhs=xn2c[:, kk * TC:(kk + 1) * TC],
                            start=(kk == 0), stop=(kk == NB - 1))
                        if kk == NB - 1:
                            inc(ins, sP, 1)
                v_dmm_list.append(val(sP))
                v_exp_list.append(mA_exp[k])

            for k in range(NB):
                gk = c * NB + k

                @block.vector
                def _(vector, c=c, k=k, gk=gk, mA_ln=mA_ln):
                    vector.wait_ge(sA, mA_ln[k])
                    v_u_list.append(inc(vector.tensor_tensor(
                        out=uu, in0=dltF[:, k * TC:(k + 1) * TC],
                        in1=xn2c[:, k * TC:(k + 1) * TC],
                        op=ALU.mult), sV, 1))
                v_u = v_u_list[-1]

                v_dbx_list_k = []

                @block.gpsimd
                def _(gpsimd, c=c, k=k, gk=gk, v_u=v_u, m_evac=m_evac):
                    gpsimd.wait_ge(sV, v_u)
                    for n in range(N):
                        if k == 0:
                            gpsimd.wait_ge(sA, m_evac[2 * n])
                        if n >= NDBX:
                            gpsimd.wait_ge(sV, v_u + n - NDBX + 1)
                        elif gk >= 1:
                            gpsimd.wait_ge(sV, v_u_list[gk - 1] + 17 + n - NDBX)
                        v_dbx_list_k.append(inc(gpsimd.tensor_tensor(
                            out=dbx2[:, (n % NDBX) * TC:(n % NDBX + 1) * TC],
                            in0=uu, in1=brep[:, n * TC:(n + 1) * TC],
                            op=ALU.mult), sG, 1))

                v_ymq = []

                @block.vector
                def _(vector, c=c, k=k, gk=gk, mA_da=mA_da, dbxms=v_dbx_list_k,
                      m_evac=m_evac, mA_cp=mA_cp):
                    for n in range(N):
                        vector.wait_ge(sA, mA_da[k][n])
                        vector.wait_ge(sG, dbxms[n])
                        if gk >= 1:
                            vector.wait_ge(sP, v_yredseg_list[gk - 1][n // 4])
                        inc(vector.tensor_tensor_scan(
                            out=hbuf[:, n * TC:(n + 1) * TC],
                            data0=da[:, (n % NDA) * TC:(n % NDA + 1) * TC],
                            data1=dbx2[:, (n % NDBX) * TC:(n % NDBX + 1) * TC],
                            initial=carry[:, k * N + n:k * N + n + 1],
                            op0=ALU.mult, op1=ALU.add), sV, 1)
                        if n % 4 == 3:
                            q = n // 4
                            if k == 0:
                                vector.wait_ge(sA, m_evac[8 * q + 7])
                            vector.wait_ge(sA, mA_cp[k][q])
                            v_ymq.append(inc(vector.tensor_tensor(
                                out=hbuf[:, q * 4 * TC:(q + 1) * 4 * TC],
                                in0=hbuf[:, q * 4 * TC:(q + 1) * 4 * TC],
                                in1=crep[:, q * 4 * TC:(q + 1) * 4 * TC],
                                op=ALU.mult), sY, 1))
                v_y = v_ymq[-1]
                v_scan_last = v_y

                if k + 1 < NB:
                    emit_delta(k + 1)

                yseg = []

                @block.tensor
                def _(tensor, c=c, k=k, gk=gk, ymq=v_ymq):
                    if gk >= 1:
                        tensor.wait_ge(sV, v_x1_list[gk - 1])
                    for q in range(4):
                        tensor.wait_ge(sY, ymq[q])
                        for n in range(q * 4, q * 4 + 4):
                            ins = tensor.matmul(out=pyc[:, 0:TC], lhsT=ident_bf[:],
                                                rhs=hbuf[:, n * TC:(n + 1) * TC],
                                                start=(n == 0), stop=(n == N - 1))
                        yseg.append(inc(ins, sP, 1))
                v_yredseg_list.append(yseg)
                v_yred_list.append(yseg[-1])
                v_yred = v_yred_list[-1]

                @block.vector
                def _(vector, c=c, k=k, slot=slot, v_yred=v_yred,
                      m_q1ev=m_q1ev):
                    if k == 0:
                        vector.wait_ge(sA, m_q1ev)
                    vector.wait_ge(sP, v_yred)
                    x1s = x1f[:, k * L + c * TC:k * L + (c + 1) * TC]
                    vector.tensor_tensor(out=tln, in0=x0t(slot, k), in1=P1B, op=ALU.mult)
                    vector.tensor_sub(out=tln, in0=tln, in1=Q1B)
                    vector.tensor_add(out=tln, in0=tln, in1=pyc[:, 0:TC])
                    v_x1_list.append(inc(vector.scalar_tensor_tensor(
                        out=x1s, in0=xn2c[:, k * TC:(k + 1) * TC],
                        scalar=dskip[:, k:k + 1], in1=tln,
                        op0=ALU.mult, op1=ALU.add), sV, 1))
            v_x1_done.append(v_x1_list[-1])
            if c + 2 < NCH:
                emit_x0_load(c + 2)

            sq_guard_lo = sums_base[c] + NB - 1
            sq_guard_next = sums_base.get(c + 1)

            @block.scalar
            def _(scalar, c=c, slot=slot, mA=mA, rep_sp=rep_sp,
                  v_v1=v_v1, xn2pre=v_xn2pre, v_bcmm=v_bcmm, dmms=v_dmm_list,
                  nsq=nsq, order=order,
                  sq_guard_lo=sq_guard_lo, sq_guard_next=sq_guard_next):
                scalar.wait_ge(sV, v_v1)
                scalar.activation(out=SR(1), in_=SR(1), func=AF.Sqrt)
                inc(scalar.activation(out=SR(2), in_=SR(2), func=AF.Sqrt), sA, 1)
                for nm in order[1:]:
                    if nm == 'exp0' or nm.startswith('exp'):
                        k = int(nm[3:])
                        scalar.wait_ge(sP, dmms[k])
                        inc(scalar.activation(
                            out=dltF[:, k * TC:(k + 1) * TC],
                            in_=pmm[k % 2][:, 0:TC],
                            func=AF.Exp, bias=bd[:, k:k + 1]), sA, 1)
                    elif nm.startswith('ln'):
                        k = int(nm[2:])
                        inc(scalar.activation(
                            out=dltF[:, k * TC:(k + 1) * TC],
                            in_=dltF[:, k * TC:(k + 1) * TC],
                            func=AF.Ln, bias=onesf[:]), sA, 1)
                    elif nm == 'bcm':
                        scalar.wait_ge(sP, v_bcmm)
                        inc(scalar.activation(out=bcm[:, c * TC:(c + 1) * TC],
                                              in_=pmm[1][0:32, 0:TC], func=AF.Identity,
                                              bias=bbc[:]), sA, 1)
                    elif nm == 'mid':
                        inc(scalar.activation(out=midfc[:, c * TC:(c + 1) * TC],
                                              in_=pmid[:, 0:TC], func=AF.Identity,
                                              scale=LORA_SCALE), sA, 1)
                    elif nm.startswith('ev'):
                        j = int(nm[2:])
                        scalar.wait_ge(sP, rep_sp[j])
                        dstb = brep if j % 2 == 0 else crep
                        inc(scalar.activation(
                            out=dstb[:, (j // 2) * TC:(j // 2 + 1) * TC],
                            in_=(prepA if j % 2 == 0 else prepB)[:, 0:TC],
                            func=AF.Identity), sA, 1)
                    elif nm == 'p1ev':
                        if c >= 1:
                            scalar.wait_ge(sV, v_x1_done[c - 1])
                        scalar.wait_ge(sP, rep_sp['p1'])
                        inc(scalar.activation(out=bca[:, 0:TC], in_=prepA[:, 0:TC],
                                              func=AF.Identity), sA, 1)
                    elif nm == 'q1ev':
                        scalar.wait_ge(sP, rep_sp['q1'])
                        inc(scalar.activation(out=bca[:, TC:2 * TC], in_=prepB[:, 0:TC],
                                              func=AF.Identity), sA, 1)
                    elif nm.startswith('sq'):
                        j = int(nm[2:])
                        if j == 0:
                            scalar.wait_ge(qX[(slot + 1) % 2], v_ldx_by_chunk[c + 1])
                        if j >= 2:
                            scalar.wait_ge(sP, sq_guard_next + j - 1)
                        else:
                            scalar.wait_ge(sP, sq_guard_lo + j)
                        inc(scalar.activation(
                            out=x2c[:, (j % 2) * TC:(j % 2 + 1) * TC],
                            in_=x0t((slot + 1) % 2, j),
                            func=AF.Square), sA, 1)
                    elif nm.startswith('cp_'):
                        k, q = (int(t) for t in nm[3:].split('_'))
                        gk = c * NB + k
                        scalar.wait_ge(sV, v_u_list[gk] + 4 * q + 4)
                        inc(scalar.activation(
                            out=carry[:, k * N + q * 4:k * N + q * 4 + 4],
                            in_=hbuf.rearrange("p (n t) -> p n t", n=N)[
                                :, q * 4:q * 4 + 4, TC - 1:TC].squeeze(2),
                            func=AF.Identity), sA, 1)
                    elif nm.startswith('da_'):
                        k, n = (int(t) for t in nm[3:].split('_'))
                        gk = c * NB + k
                        if n >= NDA:
                            scalar.wait_ge(sV, v_u_list[gk] + n - NDA + 1)
                        elif gk >= 1:
                            scalar.wait_ge(sV, v_u_list[gk - 1] + 17 + n - NDA)
                        inc(scalar.activation(
                            out=da[:, (n % NDA) * TC:(n % NDA + 1) * TC],
                            in_=dltF[:, k * TC:(k + 1) * TC], func=AF.Exp,
                            scale=aneg[:, k * N + n:k * N + n + 1]), sA, 1)
                    else:
                        raise AssertionError(nm)
            assert val(sA) == A_end, (c, val(sA), A_end)
        v_attn = val(sV)
        v_m1_prev = v_m1_by_chunk[NCH - 1]

        fc_grp_ms = []
        pj_ms = []
        x2a_all = []
        x2st_all = []
        v_bcev_last = prev_bcm_mid
        HS = FB * TC

        def h1s(s):
            return h1c[:, s * HS:(s + 1) * HS]

        mPre_base = val(sA)

        @block.scalar
        def _(scalar):
            scalar.wait_ge(sV, v_attn)
            for k in range(NB):
                inc(scalar.activation(
                    out=h1s(0)[:, k * TC:(k + 1) * TC],
                    in_=x1f[:, k * L:k * L + TC],
                    func=AF.Square), sA, 1)
        mPre_sq = [mPre_base + 1 + k for k in range(NB)]

        sums_pe = {}

        @block.tensor
        def _(tensor):
            tensor.wait_ge(sV, v_m1_prev)
            for k in range(NB):
                tensor.wait_ge(sA, mPre_sq[k])
                tensor.matmul(out=pln[0:1, 0:TC], lhsT=ones_bf[:],
                              rhs=x1f[:, k * L:k * L + TC],
                              start=(k == 0), stop=(k == NB - 1))
                inc(tensor.matmul(out=pln[0:1, TC:2 * TC], lhsT=ones_bf[:],
                                  rhs=h1s(0)[:, k * TC:(k + 1) * TC],
                                  start=(k == 0), stop=(k == NB - 1)), sP, 1)
        sums_pe[0] = val(sP)

        def emit_fstats_a(cc):
            @block.vector
            def _(vector, v_s=sums_pe[cc]):
                vector.wait_ge(sP, v_s)
                vector.tensor_scalar_mul(out=SR(0), in0=pln[0:1, 0:TC], scalar1=1.0 / D)
                vector.tensor_scalar_mul(out=SR(1), in0=pln[0:1, TC:2 * TC], scalar1=1.0 / D)
                vector.tensor_mul(out=SR(2), in0=SR(0), in1=SR(0))
                vector.tensor_sub(out=SR(1), in0=SR(1), in1=SR(2))
                inc(vector.tensor_scalar_add(out=SR(1), in0=SR(1), scalar1=EPS), sV, 1)
            return val(sV)

        def emit_fstats_b(sqrt_ms):
            @block.vector
            def _(vector, sqrt_ms=sqrt_ms):
                vector.wait_ge(sA, sqrt_ms)
                vector.reciprocal(out=SR(1), in_=SR(1))
                inc(vector.tensor_mul(out=SR(2), in0=SR(0), in1=SR(1)),
                    sV, 1)
            v_sb = val(sV)

            @block.sync
            def _(sync, v_sb=v_sb):
                sync.wait_ge(sV, v_sb)
                inc(sync.dma_start(out=statAB[0:1, :], in_=SR(1)), qB, 16)
                inc(sync.dma_start(out=statAB[1:2, :], in_=SR(2)), qB, 16)
            return val(qB)

        def emit_fouter(v_sf, bank_guard_sem, bank_guard):
            @block.tensor
            def _(tensor, v_sf=v_sf, bg=bank_guard):
                tensor.wait_ge(qB, v_sf)
                tensor.wait_ge(bank_guard_sem, bg)
                tensor.matmul(out=prepA[:, 0:TC], lhsT=sel2[:, 0:128], rhs=statAB[:],
                              start=True, stop=True)
                inc(tensor.matmul(out=prepB[:, 0:TC], lhsT=sel2[:, 128:256], rhs=statAB[:],
                                  start=True, stop=True), sP, 1)
            return val(sP)

        v_xnf_list = {}

        def emit_fapply(cc, v_outer, v_fc_guard):
            @block.vector
            def _(vector, cc=cc, v_outer=v_outer, v_fc=v_fc_guard):
                vector.wait_ge(sP, v_outer)
                if v_fc:
                    vector.wait_ge(sP, v_fc)
                for k in range(NB):
                    vector.tensor_tensor(
                        out=aptmp, in0=x1f[:, k * L + cc * TC:k * L + (cc + 1) * TC],
                        in1=prepA[:, 0:TC], op=ALU.mult)
                    inc(vector.tensor_sub(
                        out=xnfc[:, k * TC:(k + 1) * TC], in0=aptmp,
                        in1=prepB[:, 0:TC]), sV, 1)
            v_xnf_list[cc] = val(sV)

        def fcw_dst_g(wslot, lo, hi):
            return fcw[:, wslot * NB * 512:(wslot + 1) * NB * 512].rearrange(
                "p (k t) -> p k t", k=NB)[:, lo:hi]

        def fcw_src_g(g, lo, hi):
            return qfc_h.rearrange("(k p) f -> p k f", p=128)[
                :, lo:hi, g * 512:(g + 1) * 512]

        def pjw_dst_k(wslot, lo, hi):
            return pjw[:, wslot * FB * 128:(wslot + 1) * FB * 128].rearrange(
                "p (a d) -> p a d", a=FB)[:, lo:hi]

        def pjw_src_k(k, lo, hi):
            return qproj_h[:, k * 128:(k + 1) * 128].rearrange(
                "(a p) d -> p a d", p=128)[:, lo:hi]

        def qfw2_cnt(c, g):
            if c == 0 and g <= 1:
                return 16 * (g + 1)
            if g >= 2:
                return 16 * (3 + 8 * c + g - 2)
            return 16 * (9 + 8 * (c - 1) + g)

        qpw2_cnt = qfw2_cnt

        @block.scalar
        def _(scalar):
            scalar.wait_ge(sV, v_attn)
            scalar.wait_ge(qQ[0], v_quant_q0)
            scalar.wait_ge(qQ[1], v_quant_q1)
            for g in (0, 1):
                inc(scalar.dma_start(out=fcw_dst_g(g % 2, 4, 8),
                                     in_=fcw_src_g(g, 4, 8)), qFW2, 16)
            for k in (0, 1):
                inc(scalar.dma_start(out=pjw_dst_k(k % 2, 16, 32),
                                     in_=pjw_src_k(k, 16, 32)), qPW2, 16)

        v_vf0 = emit_fstats_a(0)
        pre_sqrt_base = val(sA)

        @block.scalar
        def _(scalar, v_vf0=v_vf0):
            scalar.wait_ge(sV, v_vf0)
            inc(scalar.activation(out=SR(1), in_=SR(1), func=AF.Sqrt), sA, 1)
        v_sf0 = emit_fstats_b(pre_sqrt_base + 1)
        v_out0mm = emit_fouter(v_sf0, sA, prev_prep_guards[1])
        emit_fapply(0, v_out0mm, 0)

        v_midp_prev = 0
        v_statsF_prev = v_sf0
        for c in range(NCH):
            pipe = c + 1 < NCH

            F0 = val(sA)
            order_f = []
            for f in range(8):
                order_f += ['gelu%d' % f]
                if f == 3:
                    order_f += ['ldf2']
                if f == 7:
                    order_f += ['ldf3']
            if pipe:
                order_f += ['fsq%d' % j for j in range(NB)]
            for f in range(8, FB):
                order_f += ['gelu%d' % f]
                if f % 4 == 3 and 2 + (f - 3) // 4 <= 7:
                    order_f += ['ldf%d' % (2 + (f - 3) // 4)]
                if f == 27 and pipe:
                    order_f += ['ldf8']
                if f == 31 and pipe:
                    order_f += ['ldf9']
            if pipe:
                order_f += ['fsqrt']
            for k in range(NB):
                order_f += ['x2a%d' % k]
                if k + 2 < NB:
                    order_f += ['ldp%d' % (k + 2)]
                elif pipe:
                    order_f += ['ldp%d' % (k + 2)]
            acts_only = [nm for nm in order_f
                         if not (nm.startswith('ldf') or nm.startswith('ldp'))]
            mF = {nm: F0 + 1 + i for i, nm in enumerate(acts_only)}
            F_end = F0 + len(acts_only)
            mF_gelu = [mF['gelu%d' % f] for f in range(FB)]
            mF_x2a = [mF['x2a%d' % k] for k in range(NB)]

            v_pe_f_base = val(sP)

            fcw_ld = []
            for g in range(FB // 4):
                wslot = g % 2
                guard = fc_grp_ms[-2] if len(fc_grp_ms) >= 2 else None

                @block.sync
                def _(sync, c=c, g=g, guard=guard, wslot=wslot):
                    if c == 0 and g <= 1:
                        sync.wait_ge(sV, v_attn)
                        sync.wait_ge(qQ[0], v_quant_q0)
                        sync.wait_ge(qQ[1], v_quant_q1)
                    if guard is not None:
                        sync.wait_ge(sP, guard)
                    inc(sync.dma_start(out=fcw_dst_g(wslot, 0, 4),
                                       in_=fcw_src_g(g, 0, 4)), qFW, 16)
                fcw_ld.append(val(qFW))

                for fi in range(4):
                    f = g * 4 + fi
                    bank = f % 2

                    @block.tensor
                    def _(tensor, c=c, g=g, f=f, fi=fi, wslot=wslot, bank=bank,
                          mF_gelu=mF_gelu,
                          x2a_prev=(x2a_all[-1] if x2a_all else None)):
                        tensor.wait_ge(qFW, fcw_ld[g])
                        tensor.wait_ge(qFW2, qfw2_cnt(c, g))
                        tensor.wait_ge(sV, v_xnf_list[c])
                        tensor.wait_ge(sA, v_bcev_last)
                        if f >= 2:
                            tensor.wait_ge(sA, mF_gelu[f - 2])
                        elif x2a_prev is not None:
                            tensor.wait_ge(sA, x2a_prev)
                        for k in range(NB):
                            tensor.matmul(
                                out=pmm[bank][:, 0:TC],
                                lhsT=fcw[:, (wslot * NB + k) * 512 + fi * 128:
                                         (wslot * NB + k) * 512 + (fi + 1) * 128],
                                rhs=xnfc[:, k * TC:(k + 1) * TC],
                                start=(k == 0), stop=False)
                        inc(tensor.matmul(
                            out=pmm[bank][:, 0:TC],
                            lhsT=bfcT[:, f * 128:(f + 1) * 128],
                            rhs=midfc[:, c * TC:(c + 1) * TC],
                            start=False, stop=True), sP, 1)
                fc_grp_ms.append(val(sP))
            v_fc_pe = val(sP)
            v_fcmm_list = list(range(v_pe_f_base + 1, v_pe_f_base + 1 + FB))

            if pipe:
                @block.tensor
                def _(tensor, c=c, mF=mF, v_sa_prev=v_statsF_prev):
                    tensor.wait_ge(sV, v_sa_prev)
                    for k in range(NB):
                        tensor.wait_ge(sA, mF['fsq%d' % k])
                        tensor.matmul(
                            out=pln[0:1, 0:TC], lhsT=ones_bf[:],
                            rhs=x1f[:, k * L + (c + 1) * TC:k * L + (c + 2) * TC],
                            start=(k == 0), stop=(k == NB - 1))
                        inc(tensor.matmul(
                            out=pln[0:1, TC:2 * TC], lhsT=ones_bf[:],
                            rhs=h1s((c + 1) % 2)[:, k * TC:(k + 1) * TC],
                            start=(k == 0), stop=(k == NB - 1)), sP, 1)
                sums_pe[c + 1] = val(sP)

            @block.tensor
            def _(tensor, c=c, v_midp_prev=v_midp_prev, mF_gelu=mF_gelu):
                tensor.wait_ge(sA, mF_gelu[FB - 1])
                if v_midp_prev:
                    tensor.wait_ge(sV, v_midp_prev)
                for fk in range(FB):
                    ins = tensor.matmul(
                        out=pmid[:, 0:TC],
                        lhsT=aprojT[:, fk * N:(fk + 1) * N],
                        rhs=h1s(c % 2)[:, fk * TC:(fk + 1) * TC],
                        start=(fk == 0), stop=(fk == FB - 1))
                inc(ins, sP, 1)
            v_midmm = val(sP)

            @block.vector
            def _(vector, c=c, v_midmm=v_midmm):
                vector.wait_ge(sP, v_midmm)
                inc(vector.tensor_scalar_mul(out=midpj[:], in0=pmid[:, 0:TC],
                                             scalar1=LORA_SCALE), sV, 1)
            v_midp = val(sV)
            v_midp_prev = v_midp

            if pipe:
                v_vf_n = emit_fstats_a(c + 1)

            v_x2_list = []

            def emit_proj(k, c=c, mF_x2a=mF_x2a, v_midp=v_midp):
                wslot = k % 2
                guard = pj_ms[-2] if len(pj_ms) >= 2 else None
                bank = k % 2

                @block.sync
                def _(sync, c=c, k=k, guard=guard, wslot=wslot):
                    if c == 0 and k <= 1:
                        sync.wait_ge(sV, v_attn)
                        sync.wait_ge(qQ[0], v_quant_q0)
                        sync.wait_ge(qQ[1], v_quant_q1)
                    if guard is not None:
                        sync.wait_ge(sP, guard)
                    inc(sync.dma_start(out=pjw_dst_k(wslot, 0, 16),
                                       in_=pjw_src_k(k, 0, 16)), qPW, 16)
                v_pjw = val(qPW)

                @block.tensor
                def _(tensor, c=c, k=k, bank=bank, wslot=wslot,
                      v_pjw=v_pjw, v_midp=v_midp, mF_x2a=mF_x2a):
                    tensor.wait_ge(qPW, v_pjw)
                    tensor.wait_ge(qPW2, qpw2_cnt(c, k))
                    tensor.wait_ge(sV, v_midp)
                    if k >= 2:
                        tensor.wait_ge(sA, mF_x2a[k - 2])
                    for fk in range(FB):
                        tensor.matmul(
                            out=pmm[bank][:, 0:TC],
                            lhsT=pjw[:, wslot * FB * 128 + fk * 128:
                                     wslot * FB * 128 + (fk + 1) * 128],
                            rhs=h1s(c % 2)[:, fk * TC:(fk + 1) * TC],
                            start=(fk == 0), stop=False)
                    inc(tensor.matmul(
                        out=pmm[bank][:, 0:TC],
                        lhsT=bprojT[:, k * 128:(k + 1) * 128],
                        rhs=midpj[:],
                        start=False, stop=True), sP, 1)
                pj_ms.append(val(sP))
                x2a_all.append(mF_x2a[k])

            def emit_x2(k, c=c, mF_x2a=mF_x2a):
                @block.vector
                def _(vector, c=c, k=k, mF_x2a=mF_x2a):
                    vector.wait_ge(sA, mF_x2a[k])
                    xs = x2f[:, (k % 2) * TC:(k % 2 + 1) * TC]
                    inc(vector.tensor_add(
                        out=xs, in0=xs,
                        in1=x1f[:, k * L + c * TC:k * L + (c + 1) * TC]), sV, 1)
                v_x2_list.append(val(sV))
                v_x2 = v_x2_list[-1]

                @block.sync
                def _(sync, c=c, k=k, v_x2=v_x2):
                    sync.wait_ge(sV, v_x2)
                    x2st_all.append((qO[k % 2], inc(sync.dma_start(
                        out=out_h[k * 128:(k + 1) * 128, c * TC:(c + 1) * TC],
                        in_=x2f[:, (k % 2) * TC:(k % 2 + 1) * TC]), qO[k % 2], 16)))

            emit_proj(0)
            emit_proj(1)
            if pipe:
                v_outer_n = emit_fouter(
                    emit_fstats_b(mF['fsqrt']), sV, v_xnf_list[c])
            emit_x2(0)
            emit_x2(1)
            if pipe:
                emit_fapply(c + 1, v_outer_n, v_fc_pe)
                v_statsF_prev = v_vf_n
            for k in range(2, NB):
                emit_proj(k)
                emit_x2(k)

            @block.scalar
            def _(scalar, c=c, mF=mF, order_f=order_f, fcmm=v_fcmm_list,
                  pjm=list(pj_ms[-NB:]), x2st=list(x2st_all),
                  grp_ms=list(fc_grp_ms[-8:]),
                  v_vf_n=(v_vf_n if pipe else 0)):
                for nm in order_f:
                    if nm.startswith('ldf'):
                        g = int(nm[3:])
                        if g <= 7:
                            scalar.wait_ge(sP, grp_ms[g - 2])
                            inc(scalar.dma_start(out=fcw_dst_g(g % 2, 4, 8),
                                                 in_=fcw_src_g(g, 4, 8)), qFW2, 16)
                        else:
                            gg = g - 8
                            scalar.wait_ge(sP, grp_ms[gg + 6])
                            inc(scalar.dma_start(out=fcw_dst_g(gg % 2, 4, 8),
                                                 in_=fcw_src_g(gg, 4, 8)), qFW2, 16)
                    elif nm.startswith('ldp'):
                        k = int(nm[3:])
                        if k <= 7:
                            scalar.wait_ge(sP, pjm[k - 2])
                            inc(scalar.dma_start(out=pjw_dst_k(k % 2, 16, 32),
                                                 in_=pjw_src_k(k, 16, 32)), qPW2, 16)
                        else:
                            kk = k - 8
                            scalar.wait_ge(sP, pjm[kk + 6])
                            inc(scalar.dma_start(out=pjw_dst_k(kk % 2, 16, 32),
                                                 in_=pjw_src_k(kk, 16, 32)), qPW2, 16)
                    elif nm.startswith('gelu'):
                        f = int(nm[4:])
                        scalar.wait_ge(sP, fcmm[f])
                        inc(scalar.activation(
                            out=h1s(c % 2)[:, f * TC:(f + 1) * TC],
                            in_=pmm[(f % 2)][:, 0:TC], func=AF.Gelu,
                            bias=bfc[:, f:f + 1]), sA, 1)
                    elif nm == 'fsqrt':
                        scalar.wait_ge(sV, v_vf_n)
                        inc(scalar.activation(out=SR(1), in_=SR(1), func=AF.Sqrt),
                            sA, 1)
                    elif nm.startswith('fsq'):
                        j = int(nm[3:])
                        inc(scalar.activation(
                            out=h1s((c + 1) % 2)[:, j * TC:(j + 1) * TC],
                            in_=x1f[:, j * L + (c + 1) * TC:j * L + (c + 2) * TC],
                            func=AF.Square), sA, 1)
                    elif nm.startswith('x2a'):
                        k = int(nm[3:])
                        scalar.wait_ge(sP, pjm[k])
                        idx = len(x2st) - NB + k - 2
                        if idx >= 0:
                            smv = x2st[idx]
                            scalar.wait_ge(smv[0], smv[1])
                        inc(scalar.activation(
                            out=x2f[:, (k % 2) * TC:(k % 2 + 1) * TC],
                            in_=pmm[(k % 2)][:, 0:TC],
                            func=AF.Identity, bias=bproj[:, k:k + 1]), sA, 1)
                    else:
                        raise AssertionError(nm)
            assert val(sA) == F_end, (c, val(sA), F_end)

        v_out0 = val(qO[0])
        v_out1 = val(qO[1])

        @block.vector
        def _(vector):
            vector.wait_ge(qO[0], v_out0)
            vector.wait_ge(qO[1], v_out1)

    return nc


def _mk_repc():
    import ml_dtypes
    r = np.zeros((32, 32 * 128), dtype=ml_dtypes.bfloat16)
    for j in range(32):
        row = (j // 2) if j % 2 == 0 else 16 + (j // 2)
        r[row, j * 128:(j + 1) * 128] = 1
    return r


def kernel(**inputs):
    ins = {k: np.asarray(v) for k, v in inputs.items()}
    x = ins["L0_attn_x"]
    n_cores = 8

    common = dict(
        wd=np.ascontiguousarray(ins["L0_attn_W_delta"]),
        wbc=np.ascontiguousarray(
            np.concatenate([ins["L0_attn_W_B"], ins["L0_attn_W_C"]], axis=1)),
        afc=np.ascontiguousarray(ins["L0_ffn_A_fc"].T),
        bfcT=np.ascontiguousarray(ins["L0_ffn_B_fc"].T),
        wfcT=np.ascontiguousarray(ins["L0_ffn_W_fc"].T),
        wprojT=np.ascontiguousarray(ins["L0_ffn_W_proj"].T),
        aprojT=np.ascontiguousarray(ins["L0_ffn_A_proj"].T),
        bprojT=np.ascontiguousarray(ins["L0_ffn_B_proj"].T),
        alog=np.ascontiguousarray(ins["L0_attn_A_log"]),
        gam=np.ascontiguousarray(ins["L0_attn_gamma"][:, None]),
        bet=np.ascontiguousarray(ins["L0_attn_beta"][:, None]),
        dskip=np.ascontiguousarray(ins["L0_attn_D_skip"][:, None]),
        bd=np.ascontiguousarray(ins["L0_attn_b_delta"][:, None]),
        bbc=np.ascontiguousarray(
            np.concatenate([ins["L0_attn_b_B"], ins["L0_attn_b_C"]])[:, None]),
        bfc=np.ascontiguousarray(ins["L0_ffn_b_fc"][:, None]),
        bproj=np.ascontiguousarray(ins["L0_ffn_b_proj"][:, None]),
        ident=np.eye(128, dtype=np.float32),
        sel2=np.concatenate([
            np.concatenate([np.ones((1, 128), np.float32), np.zeros((1, 128), np.float32)]),
            np.concatenate([np.zeros((1, 128), np.float32), np.ones((1, 128), np.float32)]),
        ], axis=1),
        repc=_mk_repc(),
    )

    nc = build_program()
    in_maps = []
    for b in range(n_cores):
        m = dict(common)
        m["x0"] = np.ascontiguousarray(x[b].T)
        in_maps.append(m)
    res = run_bass_kernel_spmd(nc, in_maps, core_ids=list(range(n_cores)))
    out = np.stack([np.asarray(r["out"]).T for r in res.results], axis=0)
    return np.ascontiguousarray(out.astype(np.float32))
